# version 03
# baseline (speedup 1.0000x reference)
import numpy as np
from contextlib import ExitStack

import concourse.bass as bass
import concourse.mybir as mybir
from concourse.bass_utils import run_bass_kernel_spmd

F32 = mybir.dt.float32
BF16 = mybir.dt.bfloat16
I32 = mybir.dt.int32
AF = mybir.ActivationFunctionType
ALU = mybir.AluOpType
AX = mybir.AxisListType

P = 128
R = 2048
D = 1024
F = 4096
DT = D // P
FT = F // P
ET = D // P
RT = R // P
RB = 256
NRB = R // RB
PRB = 512
NPRB = R // PRB
EPS = 1e-5
N_CORES = 8


class Sem:
    def __init__(self, nc, ctx, name):
        self.h = ctx.enter_context(nc.semaphore(name))
        self.n = 0

    def inc(self, instr, k=1):
        instr.then_inc(self.h, k)
        self.n += k
        return self.n

    def tok(self, instr, k=16):
        return (self, self.inc(instr, k))


def wdma(eng, tok):
    eng.wait_ge(tok[0].h, tok[1])


def emit_ffn(nc, name, src_hbm, dst_hbm, wq1t, wq2t, wq_ld, src_ready,
             ones_row_bf, b1h, b2h_bf, ident_bf, eps_c,
             sL, sS, sT, sV, sA, sHD, sHA, pre_body=None, final_wait=False,
             dbg=None, dbg_base=0, dump_w2_to=None):
    wq1t_v = wq1t[:]
    wq2t_v = wq2t[:]
    with ExitStack() as c:
        xrb = [c.enter_context(nc.sbuf_tensor(f"{name}_x{i}", [P, 2 * D], F32))
               for i in range(2)]
        xn = [c.enter_context(nc.sbuf_tensor(f"{name}_xn{i}", [P, 2 * D], BF16))
              for i in range(2)]
        xnT = [c.enter_context(nc.sbuf_tensor(f"{name}_xnT{i}", [P, DT * RB], BF16))
               for i in range(2)]
        hT = c.enter_context(nc.sbuf_tensor(f"{name}_hT", [P, FT * RB], BF16))
        st = c.enter_context(nc.sbuf_tensor(f"{name}_st", [P, RT * 2], F32))
        vp = c.enter_context(nc.sbuf_tensor(f"{name}_vp", [P, RT], F32))
        rst = c.enter_context(nc.sbuf_tensor(f"{name}_rst", [P, RT], F32))
        ssum = c.enter_context(nc.sbuf_tensor(f"{name}_ssum", [P, RT], F32))
        ssq = c.enter_context(nc.sbuf_tensor(f"{name}_ssq", [P, RT], F32))
        msq = c.enter_context(nc.sbuf_tensor(f"{name}_msq", [P, RT], F32))
        sqscr = c.enter_context(nc.sbuf_tensor(f"{name}_sqscr", [P, D], F32))
        jnkV = c.enter_context(nc.sbuf_tensor(f"{name}_jnkV", [P, 2], F32))
        jnkA = c.enter_context(nc.sbuf_tensor(f"{name}_jnkA", [P, 2], F32))
        ps_h = [c.enter_context(nc.psum_tensor(f"{name}_psh{i}", [P, 512], F32))
                for i in range(2)]
        ps_o = [c.enter_context(nc.psum_tensor(f"{name}_pso{i}", [P, D], F32))
                for i in range(2)]
        ps_x = [c.enter_context(nc.psum_tensor(f"{name}_psx{i}", [P, 1024], BF16))
                for i in range(2)]

        ld_x = [None] * NRB
        var_done = [[0, 0] for _ in range(NRB)]
        ssq_done = [[0, 0] for _ in range(NRB)]
        sq_done = [[0, 0] for _ in range(NRB)]
        xn_done = [[0, 0] for _ in range(NRB)]
        trp_done = [[0, 0] for _ in range(NRB)]
        xnT_done = [[0, 0] for _ in range(NRB)]
        h_mm = [[0] * FT for _ in range(NRB)]
        h_ev = [[0] * FT for _ in range(NRB)]
        o_mm = [[0, 0] for _ in range(NRB)]
        o_ev = [[0, 0] for _ in range(NRB)]
        st_done = [None] * NRB

        with nc.Block(name) as blk:
            if pre_body is not None:
                pre_body(blk)

            def load_x(sync, i):
                for tok in (src_ready or []):
                    wdma(sync, tok)
                if i >= 2:
                    wdma(sync, st_done[i - 2])
                src = src_hbm[i * RB:(i + 1) * RB, :] \
                    .rearrange("(t p) d -> p t d", p=P)
                ins = sync.dma_start(
                    out=xrb[i % 2][:].rearrange("p (t d) -> p t d", t=2), in_=src)
                ld_x[i] = sL[i % 2].tok(ins)

            def s_x01(sync):
                load_x(sync, 0)
                load_x(sync, 1)
            blk.sync(s_x01)

            for i in range(NRB):
                p = i % 2
                for rt in range(2):
                    g = 2 * i + rt
                    def a_ssq(scalar, i=i, rt=rt, g=g, p=p):
                        wdma(scalar, ld_x[i])
                        xt = xrb[p][:, rt * D:(rt + 1) * D]
                        scalar.activation(sqscr[:], xt, AF.Square,
                                          accum_out=ssq[:, g:g + 1])
                        ssq_done[i][rt] = sA.inc(scalar.drain())
                    blk.scalar(a_ssq)

                    def v_bn(vector, i=i, rt=rt, g=g, p=p):
                        wdma(vector, ld_x[i])
                        xt = xrb[p][:, rt * D:(rt + 1) * D]
                        n = sV.inc(vector.reduce_sum(ssum[:, g:g + 1], xt,
                                                     axis=AX.X))
                        vector.wait_ge(sV.h, n)
                        n = sV.inc(vector.tensor_scalar(
                            st[:, 2 * g:2 * g + 1], ssum[:, g:g + 1],
                            1.0 / 1024.0, None, ALU.mult))
                        vector.wait_ge(sV.h, n)
                        n = sV.inc(vector.scalar_tensor_tensor(
                            msq[:, g:g + 1], st[:, 2 * g:2 * g + 1], 4.0,
                            st[:, 2 * g:2 * g + 1], ALU.mult, ALU.mult))
                        vector.wait_ge(sV.h, n)
                        vector.wait_ge(sA.h, ssq_done[i][rt])
                        vector.scalar_tensor_tensor(
                            vp[:, g:g + 1], ssq[:, g:g + 1], 4.0 / 1024.0,
                            msq[:, g:g + 1], ALU.mult, ALU.subtract)
                        var_done[i][rt] = sV.inc(vector.drain())
                    blk.vector(v_bn)

                    def a_sq(scalar, i=i, rt=rt, g=g):
                        scalar.wait_ge(sV.h, var_done[i][rt])
                        scalar.activation(rst[:, g:g + 1], vp[:, g:g + 1],
                                          AF.Sqrt, bias=eps_c[:, 0:1])
                        sq_done[i][rt] = sA.inc(scalar.drain())
                    blk.scalar(a_sq)

                    def v_xn(vector, i=i, rt=rt, g=g, p=p):
                        vector.wait_ge(sA.h, sq_done[i][rt])
                        if i >= 2:
                            vector.wait_ge(sT.h, trp_done[i - 2][1])
                        n = sV.inc(vector.reciprocal(rst[:, g:g + 1],
                                                     rst[:, g:g + 1]))
                        vector.wait_ge(sV.h, n)
                        xt = xrb[p][:, rt * D:(rt + 1) * D]
                        vector.tensor_scalar(
                            xn[p][:, rt * D:(rt + 1) * D], xt,
                            st[:, 2 * g:2 * g + 1], rst[:, g:g + 1],
                            ALU.subtract, ALU.mult)
                        xn_done[i][rt] = sV.inc(vector.drain())
                    blk.vector(v_xn)

                for rt in range(2):
                    def t_tr(tensor, i=i, rt=rt, p=p):
                        tensor.wait_ge(sV.h, xn_done[i][rt])
                        if i >= 1:
                            tensor.wait_ge(sA.h, xnT_done[i - 1][rt])
                        for dt in range(DT):
                            tensor.transpose(
                                ps_x[rt][:, dt * P:(dt + 1) * P],
                                xn[p][:, rt * D + dt * P: rt * D + (dt + 1) * P],
                                ident_bf[:])
                        trp_done[i][rt] = sT.inc(tensor.drain())
                    blk.tensor(t_tr)

                    def a_xnT(scalar, i=i, rt=rt, p=p):
                        scalar.wait_ge(sT.h, trp_done[i][rt])
                        dst = xnT[p][:].rearrange("p (dt r) -> p dt r", dt=DT)[
                            :, :, rt * P:(rt + 1) * P]
                        src = ps_x[rt][:].rearrange("p (dt r) -> p dt r", dt=DT)
                        scalar.copy(dst, src)
                        xnT_done[i][rt] = sA.inc(scalar.drain())
                    blk.scalar(a_xnT)

                for fc in range(FT):
                    def t_mm1(tensor, i=i, fc=fc, p=p):
                        if fc == 0:
                            tensor.wait_ge(sA.h, xnT_done[i][1])
                            if wq_ld is not None:
                                wdma(tensor, wq_ld[0])
                        if fc >= 2:
                            prev = h_ev[i][fc - 2]
                        elif i >= 1:
                            prev = h_ev[i - 1][FT - 2 + fc]
                        else:
                            prev = None
                        if prev is not None:
                            tensor.wait_ge((sHD if fc % 2 == 0 else sHA).h, prev)
                        ins = None
                        for dt in range(DT):
                            ins = tensor.matmul(
                                ps_h[fc % 2][:, 0:RB],
                                wq1t_v[:, dt * F + fc * P: dt * F + (fc + 1) * P],
                                xnT[p][:, dt * RB:(dt + 1) * RB],
                                start=(dt == 0), stop=(dt == DT - 1))
                        h_mm[i][fc] = sT.inc(ins)
                    blk.tensor(t_mm1)

                    if fc % 2 == 0:
                        def v_hev(vector, i=i, fc=fc):
                            vector.wait_ge(sT.h, h_mm[i][fc])
                            ins = vector.tensor_scalar(
                                hT[:, fc * RB:(fc + 1) * RB],
                                ps_h[fc % 2][:, 0:RB],
                                b1h[:, fc:fc + 1], 0.0, ALU.add, ALU.max)
                            h_ev[i][fc] = sHD.inc(ins)
                        blk.vector(v_hev)
                    else:
                        def a_hev(scalar, i=i, fc=fc):
                            scalar.wait_ge(sT.h, h_mm[i][fc])
                            ins = scalar.activation(
                                hT[:, fc * RB:(fc + 1) * RB],
                                ps_h[fc % 2][:, 0:RB], AF.Relu,
                                bias=b1h[:, fc:fc + 1])
                            h_ev[i][fc] = sHA.inc(ins)
                        blk.scalar(a_hev)

                hd_after = sHD.n
                ha_after = sHA.n
                for rt in range(2):
                    def t_mm2(tensor, i=i, rt=rt, hd=hd_after, ha=ha_after):
                        if rt == 0:
                            tensor.wait_ge(sHD.h, hd)
                            tensor.wait_ge(sHA.h, ha)
                            if wq_ld is not None:
                                wdma(tensor, wq_ld[1])
                        if i >= 1:
                            tensor.wait_ge(sV.h, o_ev[i - 1][rt])
                        ins = None
                        for fc in range(FT):
                            for h in range(2):
                                ins = tensor.matmul(
                                    ps_o[rt][:, h * 512:(h + 1) * 512],
                                    hT[:, fc * RB + rt * P: fc * RB + (rt + 1) * P],
                                    wq2t_v[:, fc * D + h * 512: fc * D + (h + 1) * 512],
                                    start=(fc == 0), stop=False)
                        for h in range(2):
                            ins = tensor.matmul(
                                ps_o[rt][:, h * 512:(h + 1) * 512],
                                ones_row_bf[:],
                                b2h_bf[0:1, h * 512:(h + 1) * 512],
                                start=False, stop=True)
                        o_mm[i][rt] = sT.inc(ins)
                    blk.tensor(t_mm2)

                    def v_res(vector, i=i, rt=rt, p=p):
                        vector.wait_ge(sT.h, o_mm[i][rt])
                        xt = xrb[p][:, rt * D:(rt + 1) * D]
                        vector.tensor_tensor(xt, xt, ps_o[rt][:], ALU.add)
                        o_ev[i][rt] = sV.inc(vector.drain())
                    blk.vector(v_res)

                def a_store(scalar, i=i, p=p):
                    scalar.wait_ge(sV.h, o_ev[i][1])
                    dst = dst_hbm[i * RB:(i + 1) * RB, :] \
                        .rearrange("(t p) d -> p t d", p=P)
                    ins = scalar.dma_start(
                        out=dst, in_=xrb[p][:].rearrange("p (t d) -> p t d", t=2))
                    st_done[i] = sS[i % 2].tok(ins)
                blk.scalar(a_store)

                if i + 2 < NRB:
                    def s_pref(sync, i=i):
                        load_x(sync, i + 2)
                    blk.sync(s_pref)

            if dbg is not None:
                def a_dump(scalar):
                    scalar.wait_ge(sV.h, o_ev[NRB - 1][1])
                    b = dbg_base
                    ins = scalar.dma_start(out=dbg[:, b:b + 32], in_=st[:])
                    sS[0].inc(ins, 16)
                    ins = scalar.dma_start(out=dbg[:, b + 32:b + 48], in_=rst[:])
                    sS[0].inc(ins, 16)
                    ins = scalar.dma_start(out=dbg[:, b + 48:b + 64], in_=vp[:])
                    sS[0].inc(ins, 16)
                blk.scalar(a_dump)

            if final_wait:
                final_a = sS[0].n
                final_b = sS[1].n

                def a_final(scalar):
                    scalar.wait_ge(sS[0].h, final_a)
                    scalar.wait_ge(sS[1].h, final_b)
                blk.scalar(a_final)


def build_program(debug=False, last_phase=5):
    nc = bass.Bass("TRN2", target_bir_lowering=False)

    x_in = nc.dram_tensor("x", [R, D], F32, kind="ExternalInput").ap()
    wq_in = nc.dram_tensor("wq", [D, D], F32, kind="ExternalInput").ap()
    wk_in = nc.dram_tensor("wk", [D, D], F32, kind="ExternalInput").ap()
    wv_in = nc.dram_tensor("wv", [D, D], F32, kind="ExternalInput").ap()
    w1_in = nc.dram_tensor("w1", [F, D], F32, kind="ExternalInput").ap()
    b1_in = nc.dram_tensor("b1", [F], F32, kind="ExternalInput").ap()
    w2_in = nc.dram_tensor("w2", [D, F], F32, kind="ExternalInput").ap()
    b2_in = nc.dram_tensor("b2", [D], F32, kind="ExternalInput").ap()
    out_hbm = nc.dram_tensor("out", [R, D], F32, kind="ExternalOutput").ap()

    mid = "ExternalOutput" if debug else "Internal"
    wq1t_hbm = nc.dram_tensor("wq1t_s", [D, F], BF16, kind=mid).ap()
    wq2t_hbm = nc.dram_tensor("wq2t_s", [F, D], BF16, kind=mid).ap()
    x1_hbm = nc.dram_tensor("x1_s", [R, D], F32, kind=mid).ap()
    x2_hbm = nc.dram_tensor("x2_s", [R, D], F32, kind=mid).ap()
    dbg_hbm = nc.dram_tensor("dbg", [P, 288], F32, kind=mid).ap() if debug else None
    dbgw_hbm = nc.dram_tensor("dbgw", [1, 2048], BF16, kind=mid).ap() if debug else None
    wq2s2_hbm = nc.dram_tensor("wq2t_s2", [F, D], BF16, kind=mid).ap() if debug else None

    with ExitStack() as ctx:
        SB = lambda name, shape, dt: ctx.enter_context(nc.sbuf_tensor(name, shape, dt))

        ones_col = SB("ones_col", [P, 1], F32)
        ones_row = SB("ones_row", [1, P], F32)
        ones_row_bf = SB("ones_row_bf", [1, P], BF16)
        ident_bf = SB("ident_bf", [P, P], BF16)
        ident_f32 = SB("ident_f32", [P, P], F32)
        b1h = SB("b1h", [P, FT], F32)
        b2h_bf = SB("b2h_bf", [1, D], BF16)
        s12b = SB("s12b", [P, 2], F32)
        eps_c = SB("eps_c", [P, 2], F32)

        sL = [Sem(nc, ctx, f"sL{i}") for i in range(4)]
        sS = [Sem(nc, ctx, f"sS{i}") for i in range(2)]
        sT = Sem(nc, ctx, "sT")
        sV = Sem(nc, ctx, "sV")
        sA = Sem(nc, ctx, "sA")
        sHD = Sem(nc, ctx, "sHD")
        sHA = Sem(nc, ctx, "sHA")
        sG = Sem(nc, ctx, "sG")

        with ExitStack() as c0:
            gstage = [c0.enter_context(nc.sbuf_tensor(f"gst{i}", [P, 8192], F32))
                      for i in range(2)]
            gacc = c0.enter_context(nc.sbuf_tensor("gacc", [P, 8], F32))
            gtot = c0.enter_context(nc.sbuf_tensor("gtot", [P, 2], F32))
            sinv = c0.enter_context(nc.sbuf_tensor("sinv", [1, 2], F32))
            b1_t32 = c0.enter_context(nc.sbuf_tensor("b1_t32", [32, P], F32))
            b2raw = c0.enter_context(nc.sbuf_tensor("b2raw", [1, D], F32))
            ps0 = c0.enter_context(nc.psum_tensor("ps0", [P, 2], F32))
            ps0b = c0.enter_context(nc.psum_tensor("ps0b", [P, 32], F32))

            w1_c = w1_in.rearrange("(c t p) d -> c p t d", p=P, t=8)
            w2_c = w2_in.rearrange("(c t p) f -> c p t f", p=P, t=2)

            ld_chunk = [None] * 8
            ld_b = [None, None]
            red_done = [0] * 8

            with nc.Block("p0") as blk:
                def load_chunk(sync, k):
                    if k < 4:
                        src = w1_c[k]
                        dst = gstage[k % 2][:].rearrange("p (t d) -> p t d", t=8)
                    else:
                        src = w2_c[k - 4]
                        dst = gstage[k % 2][:].rearrange("p (t f) -> p t f", t=2)
                    ins = sync.dma_start(out=dst, in_=src)
                    ld_chunk[k] = sL[k % 2].tok(ins)

                def s_first(sync):
                    load_chunk(sync, 0)
                    load_chunk(sync, 1)
                    ins = sync.dma_start(
                        out=b1_t32[:], in_=b1_in.rearrange("(c p) -> c p", p=P))
                    ld_b[0] = sL[2].tok(ins)
                    ins = sync.dma_start(out=b2raw[:], in_=b2_in[None, :])
                    ld_b[1] = sL[3].tok(ins)
                blk.sync(s_first)

                def gp_body(gpsimd):
                    gpsimd.memset(ones_col[:], 1.0)
                    gpsimd.memset(ones_row[:], 1.0)
                    gpsimd.memset(ones_row_bf[:], 1.0)
                    gpsimd.memset(eps_c[:, 0:1], 4.0 * EPS)
                    gpsimd.memset(eps_c[:, 1:2], EPS)
                    gpsimd.memset(ident_bf[:], 0.0)
                    n = sG.inc(gpsimd.memset(ident_f32[:], 0.0))
                    gpsimd.wait_ge(sG.h, n)
                    gpsimd.affine_select(
                        out=ident_bf[:], in_=ident_bf[:],
                        compare_op=ALU.not_equal, fill=1.0, base=0,
                        pattern=[[-1, P]], channel_multiplier=1)
                    gpsimd.affine_select(
                        out=ident_f32[:], in_=ident_f32[:],
                        compare_op=ALU.not_equal, fill=1.0, base=0,
                        pattern=[[-1, P]], channel_multiplier=1)
                    sG.inc(gpsimd.drain())
                blk.gpsimd(gp_body)
                g_ready = sG.n

                for k in range(8):
                    def v_red(vector, k=k):
                        wdma(vector, ld_chunk[k])
                        ins = vector.reduce_sum(gacc[:, k:k + 1], gstage[k % 2][:],
                                                axis=AX.X, apply_absolute_value=True)
                        red_done[k] = sV.inc(ins)
                    blk.vector(v_red)
                    if k + 2 < 8:
                        def s_pref(sync, k=k):
                            sync.wait_ge(sV.h, red_done[k])
                            load_chunk(sync, k + 2)
                        blk.sync(s_pref)

                def v_tot(vector):
                    vector.wait_ge(sV.h, red_done[7])
                    vector.reduce_sum(gtot[:, 0:1], gacc[:, 0:4], axis=AX.X)
                    vector.reduce_sum(gtot[:, 1:2], gacc[:, 4:8], axis=AX.X)
                    sV.inc(vector.drain())
                blk.vector(v_tot)
                v_tot_n = sV.n

                def t_sum(tensor):
                    tensor.wait_ge(sV.h, v_tot_n)
                    tensor.wait_ge(sG.h, g_ready)
                    ins = tensor.matmul(ps0[0:1, 0:2], ones_col[:], gtot[:],
                                        start=True, stop=True)
                    sT.inc(ins)
                blk.tensor(t_sum)
                t_sum_n = sT.n

                def v_sinv(vector):
                    vector.wait_ge(sT.h, t_sum_n)
                    n = sV.inc(vector.tensor_copy(sinv[:], ps0[0:1, 0:2]))
                    vector.wait_ge(sV.h, n)
                    n = sV.inc(vector.tensor_scalar(
                        sinv[:], sinv[:], 1.0 / 4194304.0, EPS,
                        ALU.mult, ALU.add))
                    vector.wait_ge(sV.h, n)
                    vector.reciprocal(sinv[:], sinv[:])
                    sV.inc(vector.drain())
                blk.vector(v_sinv)
                v_sinv_n = sV.n

                def t_bc(tensor):
                    tensor.wait_ge(sV.h, v_sinv_n)
                    tensor.matmul(ps0[:, 0:2], ones_row[:], sinv[:],
                                  start=True, stop=True)
                    wdma(tensor, ld_b[0])
                    tensor.transpose(ps0b[:, 0:32], b1_t32[:],
                                     ident_f32[0:32, 0:32])
                    sT.inc(tensor.drain())
                    sT.inc(tensor.nop())
                blk.tensor(t_bc)
                t_bc_n = sT.n

                def v_fin(vector):
                    vector.wait_ge(sT.h, t_bc_n)
                    vector.tensor_copy(s12b[:], ps0[:, 0:2])
                    vector.tensor_scalar(b1h[:], ps0b[:, 0:32], 0.5, None, ALU.mult)
                    wdma(vector, ld_b[1])
                    vector.tensor_scalar(b2h_bf[:], b2raw[:], 0.5, None,
                                         ALU.mult)
                    sV.inc(vector.drain())
                blk.vector(v_fin)
                v_fin_n = sV.n

                if debug:
                    def a_dbg(scalar):
                        scalar.wait_ge(sV.h, v_fin_n)
                        for dst_ap, src_ap in (
                                (dbg_hbm[:, 0:2], s12b[:]),
                                (dbg_hbm[:, 2:34], b1h[:]),
                                (dbg_hbm[:, 34:42], gacc[:]),
                                (dbg_hbm[:, 42:44], gtot[:]),
                                (dbg_hbm[0:1, 44:46], sinv[:])):
                            sS[0].inc(scalar.dma_start(out=dst_ap, in_=src_ap), 16)
                    blk.scalar(a_dbg)

        if last_phase < 1:
            return nc
        with ExitStack() as c12:
            wq1t = c12.enter_context(nc.sbuf_tensor("wq1t", [P, DT * F], BF16))
            wq2t = c12.enter_context(nc.sbuf_tensor("wq2t", [P, FT * D], BF16))
            wq1t_vv = wq1t[:].rearrange("p (dt f) -> p dt f", dt=DT)
            wq2t_vv = wq2t[:].rearrange("p (ft d) -> p ft d", ft=FT)

            with ExitStack() as c1:
                wstage = [c1.enter_context(nc.sbuf_tensor(f"wst{i}", [P, 1024], F32))
                          for i in range(2)]
                wqi = c1.enter_context(nc.sbuf_tensor("wqi", [P, 1024], I32))
                wqn = [c1.enter_context(nc.sbuf_tensor(f"wqn{i}", [P, 1024], BF16))
                       for i in range(2)]
                psw = [c1.enter_context(nc.psum_tensor(f"psw{i}", [P, 1024], BF16))
                       for i in range(2)]

                NTILES = 64
                ld_w = [None] * NTILES
                q_done = [0] * NTILES
                tr_done = [0] * NTILES
                ev_done = [0] * NTILES

                def w_src(t):
                    if t < 32:
                        return w1_in[t * P:(t + 1) * P, :]
                    u = t - 32
                    dt, fs = u // 4, u % 4
                    return w2_in[dt * P:(dt + 1) * P, fs * 1024:(fs + 1) * 1024]

                with nc.Block("p1") as blk:
                    def load_w(sync, t):
                        ins = sync.dma_start(out=wstage[t % 2][:], in_=w_src(t))
                        ld_w[t] = sL[t % 2].tok(ins)

                    def s_first(sync):
                        load_w(sync, 0)
                        load_w(sync, 1)
                    blk.sync(s_first)

                    for t in range(NTILES):
                        def v_quant(vector, t=t):
                            wdma(vector, ld_w[t])
                            if t >= 2:
                                vector.wait_ge(sT.h, tr_done[t - 2])
                            sc = s12b[:, 0:1] if t < 32 else s12b[:, 1:2]
                            n = sV.inc(vector.tensor_scalar(
                                wqi[:], wstage[t % 2][:], sc, None, ALU.mult))
                            vector.wait_ge(sV.h, n)
                            vector.tensor_scalar(wqn[t % 2][:], wqi[:],
                                                 1.0, -1.0, ALU.min, ALU.max)
                            q_done[t] = sV.inc(vector.drain())
                        blk.vector(v_quant)

                        def t_tr(tensor, t=t):
                            tensor.wait_ge(sV.h, q_done[t])
                            if t >= 2:
                                prev = ev_done[t - 2]
                                tensor.wait_ge(
                                    (sHD if (t - 2) % 2 == 0 else sHA).h, prev)
                            for j in range(8):
                                tensor.transpose(
                                    psw[t % 2][:, j * P:(j + 1) * P],
                                    wqn[t % 2][:, j * P:(j + 1) * P],
                                    ident_bf[:])
                            tr_done[t] = sT.inc(tensor.drain())
                        blk.tensor(t_tr)

                        if t < 32:
                            dest = wq1t_vv[:, :, t * P:(t + 1) * P]
                        else:
                            u = t - 32
                            dt, fs = u // 4, u % 4
                            dest = wq2t_vv[:, fs * 8:(fs + 1) * 8, dt * P:(dt + 1) * P]
                        src = psw[t % 2][:].rearrange("p (j c) -> p j c", j=8)

                        if t % 2 == 0:
                            def v_ev(vector, t=t, dest=dest, src=src):
                                vector.wait_ge(sT.h, tr_done[t])
                                vector.tensor_copy(dest, src)
                                ev_done[t] = sHD.inc(vector.drain())
                            blk.vector(v_ev)
                        else:
                            def a_ev(scalar, t=t, dest=dest, src=src):
                                scalar.wait_ge(sT.h, tr_done[t])
                                scalar.copy(dest, src)
                                ev_done[t] = sHA.inc(scalar.drain())
                            blk.scalar(a_ev)

                        if t + 2 < NTILES:
                            def s_pref(sync, t=t):
                                sync.wait_ge(sV.h, q_done[t])
                                load_w(sync, t + 2)
                            blk.sync(s_pref)

                    hd_total = sHD.n
                    ha_total = sHA.n

                    def a_wstore(scalar):
                        scalar.wait_ge(sHD.h, hd_total)
                        scalar.wait_ge(sHA.h, ha_total)
                        ins = scalar.dma_start(
                            out=wq1t_hbm.rearrange("(a p) f -> p a f", p=P),
                            in_=wq1t_vv)
                        sS[0].inc(ins, 16)
                        ins = scalar.dma_start(
                            out=wq2t_hbm.rearrange("(a p) d -> p a d", p=P),
                            in_=wq2t_vv)
                        sS[1].inc(ins, 16)
                    blk.scalar(a_wstore)

            if last_phase < 2:
                return nc
            emit_ffn(nc, "p2", x_in, x1_hbm, wq1t, wq2t, None, None,
                     ones_row_bf, b1h, b2h_bf, ident_bf, eps_c,
                     sL, sS, sT, sV, sA, sHD, sHA,
                     dbg=dbg_hbm, dbg_base=48)

        if last_phase < 3:
            return nc
        with ExitStack() as c34:
            qT = c34.enter_context(nc.sbuf_tensor("qT", [P, ET * R], BF16))
            kT = c34.enter_context(nc.sbuf_tensor("kT", [P, ET * R], BF16))
            vN = c34.enter_context(nc.sbuf_tensor("vN", [P, RT * D], BF16))

            with ExitStack() as c3:
                wqb = c3.enter_context(nc.sbuf_tensor("wqb", [P, DT * D], BF16))
                wkb = c3.enter_context(nc.sbuf_tensor("wkb", [P, DT * D], BF16))
                wvb = c3.enter_context(nc.sbuf_tensor("wvb", [P, DT * D], BF16))
                wst3 = [c3.enter_context(nc.sbuf_tensor(f"wst3{i}", [P, 2048], F32))
                        for i in range(2)]
                xrb3 = [c3.enter_context(nc.sbuf_tensor(f"xrb3{i}", [P, 2 * D], F32))
                        for i in range(2)]
                xn3 = [c3.enter_context(nc.sbuf_tensor(f"xn3{i}", [P, 2 * D], BF16))
                       for i in range(2)]
                xn3T = [c3.enter_context(nc.sbuf_tensor(f"xn3T{i}", [P, DT * RB],
                                                        BF16))
                        for i in range(2)]
                st3 = c3.enter_context(nc.sbuf_tensor("st3", [P, RT * 2], F32))
                vp3 = c3.enter_context(nc.sbuf_tensor("vp3", [P, RT], F32))
                rst3 = c3.enter_context(nc.sbuf_tensor("rst3", [P, RT], F32))
                ssum3 = c3.enter_context(nc.sbuf_tensor("ssum3", [P, RT], F32))
                ssq3 = c3.enter_context(nc.sbuf_tensor("ssq3", [P, RT], F32))
                msq3 = c3.enter_context(nc.sbuf_tensor("msq3", [P, RT], F32))
                sqscr3 = c3.enter_context(nc.sbuf_tensor("sqscr3", [P, D], F32))
                jnkV3 = c3.enter_context(nc.sbuf_tensor("jnkV3", [P, 2], F32))
                jnkA3 = c3.enter_context(nc.sbuf_tensor("jnkA3", [P, 2], F32))
                ps_qk = [c3.enter_context(nc.psum_tensor(f"psqk{i}", [P, 512], F32))
                         for i in range(2)]
                ps_v = [c3.enter_context(nc.psum_tensor(f"psv{i}", [P, D], F32))
                        for i in range(2)]
                ps_x3 = [c3.enter_context(nc.psum_tensor(f"psx3{i}", [P, 1024], BF16))
                         for i in range(2)]

                x1_ready = [(sS[0], sS[0].n), (sS[1], sS[1].n)]
                NWC = 12
                ld_wqkv = [None] * NWC
                cast_done = [0] * NWC
                ld_x1 = [None] * NRB
                xn_done3 = [[0, 0] for _ in range(NRB)]
                var_done3 = [[0, 0] for _ in range(NRB)]
                ssq_done3 = [[0, 0] for _ in range(NRB)]
                sq_done3 = [[0, 0] for _ in range(NRB)]
                trp_done3 = [[0, 0] for _ in range(NRB)]
                xnT_done3 = [[0, 0] for _ in range(NRB)]
                qk_mm = [[0] * 16 for _ in range(NRB)]
                qk_ev = [[0] * 16 for _ in range(NRB)]
                v_mm = [[0, 0] for _ in range(NRB)]
                v_ev = [[0, 0] for _ in range(NRB)]

                with nc.Block("p3") as blk:
                    def load_wqkv(sync, t):
                        w_in_ap = (wq_in, wk_in, wv_in)[t // 4]
                        ch = t % 4
                        src = w_in_ap[ch * 2 * P:(ch + 1) * 2 * P, :] \
                            .rearrange("(a p) d -> p a d", p=P)
                        ins = sync.dma_start(
                            out=wst3[t % 2][:].rearrange("p (a d) -> p a d", a=2),
                            in_=src)
                        ld_wqkv[t] = sL[2 + t % 2].tok(ins)

                    def s_l0(sync):
                        load_wqkv(sync, 0)
                        load_wqkv(sync, 1)
                    blk.sync(s_l0)

                    for t in range(NWC):
                        def v_cast(vector, t=t):
                            wdma(vector, ld_wqkv[t])
                            dst = (wqb, wkb, wvb)[t // 4]
                            ch = t % 4
                            vector.tensor_copy(
                                dst[:, ch * 2048:(ch + 1) * 2048],
                                wst3[t % 2][:])
                            cast_done[t] = sV.inc(vector.drain())
                        blk.vector(v_cast)
                        if t + 2 < NWC:
                            def s_pref(sync, t=t):
                                sync.wait_ge(sV.h, cast_done[t])
                                load_wqkv(sync, t + 2)
                            blk.sync(s_pref)

                    def load_x1(sync, j):
                        for tok in x1_ready:
                            wdma(sync, tok)
                        if j >= 2:
                            sync.wait_ge(sV.h, xn_done3[j - 2][1])
                        src = x1_hbm[j * RB:(j + 1) * RB, :] \
                            .rearrange("(t p) d -> p t d", p=P)
                        ins = sync.dma_start(
                            out=xrb3[j % 2][:].rearrange("p (t d) -> p t d", t=2),
                            in_=src)
                        ld_x1[j] = sL[j % 2].tok(ins)

                    def s_x01(sync):
                        load_x1(sync, 0)
                        load_x1(sync, 1)
                    blk.sync(s_x01)

                    for j in range(NRB):
                        p = j % 2
                        for rt in range(2):
                            g = 2 * j + rt
                            def a_ssq(scalar, j=j, rt=rt, g=g, p=p):
                                wdma(scalar, ld_x1[j])
                                xt = xrb3[p][:, rt * D:(rt + 1) * D]
                                scalar.activation(sqscr3[:], xt, AF.Square,
                                                  accum_out=ssq3[:, g:g + 1])
                                ssq_done3[j][rt] = sA.inc(scalar.drain())
                            blk.scalar(a_ssq)

                            def v_bn(vector, j=j, rt=rt, g=g, p=p):
                                wdma(vector, ld_x1[j])
                                xt = xrb3[p][:, rt * D:(rt + 1) * D]
                                n = sV.inc(vector.reduce_sum(
                                    ssum3[:, g:g + 1], xt, axis=AX.X))
                                vector.wait_ge(sV.h, n)
                                n = sV.inc(vector.tensor_scalar(
                                    st3[:, 2 * g:2 * g + 1], ssum3[:, g:g + 1],
                                    1.0 / 1024.0, None, ALU.mult))
                                vector.wait_ge(sV.h, n)
                                n = sV.inc(vector.scalar_tensor_tensor(
                                    msq3[:, g:g + 1], st3[:, 2 * g:2 * g + 1], 1.0,
                                    st3[:, 2 * g:2 * g + 1], ALU.mult, ALU.mult))
                                vector.wait_ge(sV.h, n)
                                vector.wait_ge(sA.h, ssq_done3[j][rt])
                                vector.scalar_tensor_tensor(
                                    vp3[:, g:g + 1], ssq3[:, g:g + 1], 1.0 / 1024.0,
                                    msq3[:, g:g + 1], ALU.mult, ALU.subtract)
                                var_done3[j][rt] = sV.inc(vector.drain())
                            blk.vector(v_bn)

                            def a_sq(scalar, j=j, rt=rt, g=g):
                                scalar.wait_ge(sV.h, var_done3[j][rt])
                                scalar.activation(rst3[:, g:g + 1],
                                                  vp3[:, g:g + 1], AF.Sqrt,
                                                  bias=eps_c[:, 1:2])
                                sq_done3[j][rt] = sA.inc(scalar.drain())
                            blk.scalar(a_sq)

                            def v_xn(vector, j=j, rt=rt, g=g, p=p):
                                vector.wait_ge(sA.h, sq_done3[j][rt])
                                if j >= 2:
                                    vector.wait_ge(sT.h, trp_done3[j - 2][1])
                                n = sV.inc(vector.reciprocal(rst3[:, g:g + 1],
                                                             rst3[:, g:g + 1]))
                                vector.wait_ge(sV.h, n)
                                xt = xrb3[p][:, rt * D:(rt + 1) * D]
                                vector.tensor_scalar(
                                    xn3[p][:, rt * D:(rt + 1) * D], xt,
                                    st3[:, 2 * g:2 * g + 1], rst3[:, g:g + 1],
                                    ALU.subtract, ALU.mult)
                                xn_done3[j][rt] = sV.inc(vector.drain())
                            blk.vector(v_xn)

                        for rt in range(2):
                            def t_tr(tensor, j=j, rt=rt, p=p):
                                tensor.wait_ge(sV.h, xn_done3[j][rt])
                                if j >= 1:
                                    tensor.wait_ge(sA.h, xnT_done3[j - 1][rt])
                                for dt in range(DT):
                                    tensor.transpose(
                                        ps_x3[rt][:, dt * P:(dt + 1) * P],
                                        xn3[p][:, rt * D + dt * P:
                                               rt * D + (dt + 1) * P],
                                        ident_bf[:])
                                trp_done3[j][rt] = sT.inc(tensor.drain())
                            blk.tensor(t_tr)

                            def a_xnT(scalar, j=j, rt=rt, p=p):
                                scalar.wait_ge(sT.h, trp_done3[j][rt])
                                dst = xn3T[p][:].rearrange(
                                    "p (dt r) -> p dt r", dt=DT)[
                                    :, :, rt * P:(rt + 1) * P]
                                src = ps_x3[rt][:].rearrange(
                                    "p (dt r) -> p dt r", dt=DT)
                                scalar.copy(dst, src)
                                xnT_done3[j][rt] = sA.inc(scalar.drain())
                            blk.scalar(a_xnT)

                        for which in range(2):
                            wb = (wqb, wkb)[which]
                            dstT = (qT, kT)[which]
                            for et in range(ET):
                                idx = which * 8 + et
                                def t_qk(tensor, j=j, which=which, et=et, idx=idx,
                                         p=p, wb=wb):
                                    if idx == 0:
                                        tensor.wait_ge(sA.h, xnT_done3[j][1])
                                    tensor.wait_ge(sV.h, cast_done[which * 4 + 3])
                                    if idx >= 2:
                                        tensor.wait_ge(sV.h, qk_ev[j][idx - 2])
                                    elif j >= 1:
                                        tensor.wait_ge(sV.h, qk_ev[j - 1][14 + idx])
                                    ins = None
                                    for dt in range(DT):
                                        ins = tensor.matmul(
                                            ps_qk[idx % 2][:, 0:RB],
                                            wb[:, dt * D + et * P:
                                               dt * D + (et + 1) * P],
                                            xn3T[p][:, dt * RB:(dt + 1) * RB],
                                            start=(dt == 0), stop=(dt == DT - 1))
                                    qk_mm[j][idx] = sT.inc(ins)
                                blk.tensor(t_qk)

                                def v_qkev(vector, j=j, et=et, idx=idx, dstT=dstT):
                                    vector.wait_ge(sT.h, qk_mm[j][idx])
                                    ins = vector.tensor_copy(
                                        dstT[:, et * R + j * RB:
                                             et * R + (j + 1) * RB],
                                        ps_qk[idx % 2][:, 0:RB])
                                    qk_ev[j][idx] = sV.inc(ins)
                                blk.vector(v_qkev)

                        for rt in range(2):
                            def t_v(tensor, j=j, rt=rt, p=p):
                                tensor.wait_ge(sV.h, cast_done[11])
                                if j >= 1:
                                    tensor.wait_ge(sV.h, v_ev[j - 1][rt])
                                ins = None
                                for dt in range(DT):
                                    for h in range(2):
                                        ins = tensor.matmul(
                                            ps_v[rt % 2][:, h * 512:(h + 1) * 512],
                                            xn3T[p][:, dt * RB + rt * P:
                                                   dt * RB + (rt + 1) * P],
                                            wvb[:, dt * D + h * 512:
                                                dt * D + (h + 1) * 512],
                                            start=(dt == 0), stop=(dt == DT - 1))
                                v_mm[j][rt] = sT.inc(ins)
                            blk.tensor(t_v)

                            def v_vev(vector, j=j, rt=rt):
                                vector.wait_ge(sT.h, v_mm[j][rt])
                                g = 2 * j + rt
                                ins = vector.tensor_copy(
                                    vN[:, g * D:(g + 1) * D], ps_v[rt % 2][:])
                                v_ev[j][rt] = sV.inc(ins)
                            blk.vector(v_vev)

                        if j + 2 < NRB:
                            def s_xpref(sync, j=j):
                                load_x1(sync, j + 2)
                            blk.sync(s_xpref)

                    if dbg_hbm is not None:
                        _vlast = v_ev[NRB - 1][1]

                        def a_dump3(scalar):
                            scalar.wait_ge(sV.h, _vlast)
                            ins = scalar.dma_start(out=dbg_hbm[:, 224:256], in_=st3[:])
                            sS[0].inc(ins, 16)
                            ins = scalar.dma_start(out=dbg_hbm[:, 256:272], in_=rst3[:])
                            sS[0].inc(ins, 16)
                        blk.scalar(a_dump3)

            if last_phase < 4:
                return nc
            with ExitStack() as c4:
                x1t = [c4.enter_context(nc.sbuf_tensor(f"x1t{i}", [P, D], F32))
                       for i in range(2)]
                jnkV4 = c4.enter_context(nc.sbuf_tensor("jnkV4", [P, 2], F32))
                jnkA4 = c4.enter_context(nc.sbuf_tensor("jnkA4", [P, 2], F32))
                P_sb = [c4.enter_context(nc.sbuf_tensor(f"Psb{i}", [P, R], BF16))
                        for i in range(2)]
                PT_sb = c4.enter_context(nc.sbuf_tensor("PTsb", [P, R], BF16))
                negmax = c4.enter_context(nc.sbuf_tensor("negmax", [P, RT], F32))
                sumx = c4.enter_context(nc.sbuf_tensor("sumx", [P, RT], F32))
                rs = c4.enter_context(nc.sbuf_tensor("rs", [P, RT], F32))
                ps_s = c4.enter_context(nc.psum_tensor("ps_s", [P, R], F32))
                ps_pt = c4.enter_context(nc.psum_tensor("ps_pt", [P, R], BF16))
                ps_av = c4.enter_context(nc.psum_tensor("ps_av", [P, D], F32))

                NJT = RT
                ld_x1t = [None] * NJT
                sc_done = [0] * NJT
                max_done = [0] * NJT
                exp_done = [0] * NJT
                rs_done = [0] * NJT
                ptt_done = [0] * NJT
                pte_done = [0] * NJT
                av_done = [0] * NJT
                ave_done = [0] * NJT
                st_x2 = [None] * NJT

                with nc.Block("p4") as blk:
                    def load_x1t(sync, jt):
                        if jt >= 2:
                            wdma(sync, st_x2[jt - 2])
                        ins = sync.dma_start(out=x1t[jt % 2][:],
                                             in_=x1_hbm[jt * P:(jt + 1) * P, :])
                        ld_x1t[jt] = sL[jt % 2].tok(ins)

                    def s_x01(sync):
                        load_x1t(sync, 0)
                        load_x1t(sync, 1)
                    blk.sync(s_x01)

                    for jt in range(NJT + 1):
                        if jt >= 1:
                            def t_pt(tensor, jt=jt):
                                tensor.wait_ge(sA.h, exp_done[jt - 1])
                                if jt >= 2:
                                    tensor.wait_ge(sA.h, pte_done[jt - 2])
                                for it in range(RT):
                                    tensor.transpose(
                                        ps_pt[:, it * P:(it + 1) * P],
                                        P_sb[(jt - 1) % 2][:, it * P:(it + 1) * P],
                                        ident_bf[:])
                                ptt_done[jt - 1] = sT.inc(tensor.drain())
                            blk.tensor(t_pt)

                        if jt < NJT:
                            def t_sc(tensor, jt=jt):
                                if jt >= 1:
                                    tensor.wait_ge(sA.h, exp_done[jt - 1])
                                ins = None
                                for seg in range(4):
                                    for et in range(ET):
                                        ins = tensor.matmul(
                                            ps_s[:, seg * 512:(seg + 1) * 512],
                                            kT[:, et * R + jt * P:
                                               et * R + (jt + 1) * P],
                                            qT[:, et * R + seg * 512:
                                               et * R + (seg + 1) * 512],
                                            start=(et == 0), stop=(et == ET - 1))
                                sc_done[jt] = sT.inc(ins)
                            blk.tensor(t_sc)

                            def v_max(vector, jt=jt):
                                vector.wait_ge(sT.h, sc_done[jt])
                                n = sV.inc(vector.reduce_max(
                                    negmax[:, jt:jt + 1], ps_s[:],
                                    axis=AX.X, negate=True))
                                vector.wait_ge(sV.h, n)
                                vector.tensor_scalar(
                                    negmax[:, jt:jt + 1], negmax[:, jt:jt + 1],
                                    1.0 / 32.0, None, ALU.mult)
                                max_done[jt] = sV.inc(vector.drain())
                            blk.vector(v_max)

                        def a_body(scalar, jt=jt):
                            if jt >= 1:
                                scalar.wait_ge(sT.h, ptt_done[jt - 1])
                                if jt >= 2:
                                    scalar.wait_ge(sT.h, av_done[jt - 2])
                                scalar.copy(PT_sb[:], ps_pt[:])
                                pte_done[jt - 1] = sA.inc(scalar.drain())
                            if jt < NJT:
                                scalar.wait_ge(sV.h, max_done[jt])
                                if jt >= 2:
                                    scalar.wait_ge(sT.h, ptt_done[jt - 2])
                                scalar.activation(
                                    P_sb[jt % 2][:], ps_s[:], AF.Exp,
                                    bias=negmax[:, jt:jt + 1], scale=1.0 / 32.0,
                                    accum_out=sumx[:, jt:jt + 1])
                                exp_done[jt] = sA.inc(scalar.drain())
                        blk.scalar(a_body)

                        if jt >= 1:
                            def t_av(tensor, jt=jt):
                                tensor.wait_ge(sA.h, pte_done[jt - 1])
                                if jt >= 2:
                                    tensor.wait_ge(sV.h, ave_done[jt - 2])
                                ins = None
                                for it in range(RT):
                                    for h in range(2):
                                        ins = tensor.matmul(
                                            ps_av[:, h * 512:(h + 1) * 512],
                                            PT_sb[:, it * P:(it + 1) * P],
                                            vN[:, it * D + h * 512:
                                               it * D + (h + 1) * 512],
                                            start=(it == 0), stop=(it == RT - 1))
                                av_done[jt - 1] = sT.inc(ins)
                            blk.tensor(t_av)

                            def v_ave(vector, jt=jt):
                                jj = jt - 1
                                vector.wait_ge(sA.h, exp_done[jj])
                                ins = vector.reciprocal(rs[:, jj:jj + 1],
                                                        sumx[:, jj:jj + 1])
                                rs_done[jj] = sV.inc(ins)
                                vector.wait_ge(sV.h, rs_done[jj])
                                vector.wait_ge(sT.h, av_done[jj])
                                wdma(vector, ld_x1t[jj])
                                vector.scalar_tensor_tensor(
                                    x1t[jj % 2][:], ps_av[:], rs[:, jj:jj + 1],
                                    x1t[jj % 2][:], ALU.mult, ALU.add)
                                ave_done[jj] = sV.inc(vector.drain())
                            blk.vector(v_ave)

                            def a_st(scalar, jt=jt):
                                jj = jt - 1
                                scalar.wait_ge(sV.h, ave_done[jj])
                                ins = scalar.dma_start(
                                    out=x2_hbm[jj * P:(jj + 1) * P, :],
                                    in_=x1t[jj % 2][:])
                                st_x2[jj] = sS[jj % 2].tok(ins)
                            blk.scalar(a_st)

                            if jt + 1 < NJT:
                                def s_pref(sync, jt=jt):
                                    load_x1t(sync, jt + 1)
                                blk.sync(s_pref)

                    if dbg_hbm is not None:
                        _vlast4 = ave_done[NJT - 1]

                        def a_dump4(scalar):
                            scalar.wait_ge(sV.h, _vlast4)
                            ins = scalar.dma_start(out=dbg_hbm[:, 176:192], in_=negmax[:])
                            sS[0].inc(ins, 16)
                            ins = scalar.dma_start(out=dbg_hbm[:, 192:208], in_=sumx[:])
                            sS[0].inc(ins, 16)
                            ins = scalar.dma_start(out=dbg_hbm[:, 208:224], in_=rs[:])
                            sS[0].inc(ins, 16)
                        blk.scalar(a_dump4)

        if last_phase < 5:
            return nc
        with ExitStack() as c5:
            wq1t5 = c5.enter_context(nc.sbuf_tensor("wq1t5", [P, DT * F], BF16))
            wq2t5 = c5.enter_context(nc.sbuf_tensor("wq2t5", [P, FT * D], BF16))
            x2_ready = [(sS[0], sS[0].n), (sS[1], sS[1].n)]
            wq_ld = [None, None]

            def pre_body(blk):
                if dbg_hbm is not None:
                    def a_dumpw(scalar):
                        ins = scalar.dma_start(out=dbgw_hbm[0:1, 0:D], in_=b2h_bf[:])
                        sS[0].inc(ins, 16)
                        ins = scalar.dma_start(out=dbgw_hbm[0:1, D:D + P],
                                               in_=ones_row_bf[:])
                        sS[0].inc(ins, 16)
                    blk.scalar(a_dumpw)

                def s_wq(sync):
                    for tok in x2_ready:
                        wdma(sync, tok)
                    ins = sync.dma_start(
                        out=wq1t5[:].rearrange("p (a f) -> p a f", a=DT),
                        in_=wq1t_hbm.rearrange("(a p) f -> p a f", p=P))
                    wq_ld[0] = sL[2].tok(ins)
                    ins = sync.dma_start(
                        out=wq2t5[:].rearrange("p (a d) -> p a d", a=FT),
                        in_=wq2t_hbm.rearrange("(a p) d -> p a d", p=P))
                    wq_ld[1] = sL[3].tok(ins)
                blk.sync(s_wq)

            emit_ffn(nc, "p5", x2_hbm, out_hbm, wq1t5, wq2t5, wq_ld, x2_ready,
                     ones_row_bf, b1h, b2h_bf, ident_bf, eps_c,
                     sL, sS, sT, sV, sA, sHD, sHA,
                     pre_body=pre_body, final_wait=True,
                     dbg=dbg_hbm, dbg_base=112)

    return nc


_NC_CACHE = {}


def kernel(L0_attn_x, L0_attn_W_query, L0_attn_W_key, L0_attn_W_value,
           L0_ffn_W1, L0_ffn_b1, L0_ffn_W2, L0_ffn_b2):
    x = np.ascontiguousarray(np.asarray(L0_attn_x, dtype=np.float32))
    wq = np.ascontiguousarray(np.asarray(L0_attn_W_query, dtype=np.float32))
    wk = np.ascontiguousarray(np.asarray(L0_attn_W_key, dtype=np.float32))
    wv = np.ascontiguousarray(np.asarray(L0_attn_W_value, dtype=np.float32))
    w1 = np.ascontiguousarray(np.asarray(L0_ffn_W1, dtype=np.float32))
    b1 = np.ascontiguousarray(np.asarray(L0_ffn_b1, dtype=np.float32))
    w2 = np.ascontiguousarray(np.asarray(L0_ffn_W2, dtype=np.float32))
    b2 = np.ascontiguousarray(np.asarray(L0_ffn_b2, dtype=np.float32))

    if "nc" not in _NC_CACHE:
        _NC_CACHE["nc"] = build_program()
    nc = _NC_CACHE["nc"]

    in_maps = [
        {"x": x[c], "wq": wq, "wk": wk, "wv": wv,
         "w1": w1, "b1": b1, "w2": w2, "b2": b2}
        for c in range(N_CORES)
    ]
    res = run_bass_kernel_spmd(nc, in_maps, core_ids=list(range(N_CORES)))
    return np.stack([np.asarray(res.results[c]["out"], dtype=np.float32)
                     for c in range(N_CORES)], axis=0)


# version 04
# speedup vs baseline: 1.0035x; 1.0035x over previous
import numpy as np
from contextlib import ExitStack

import concourse.bass as bass
import concourse.mybir as mybir
from concourse.bass_utils import run_bass_kernel_spmd

F32 = mybir.dt.float32
BF16 = mybir.dt.bfloat16
I32 = mybir.dt.int32
AF = mybir.ActivationFunctionType
ALU = mybir.AluOpType
AX = mybir.AxisListType

P = 128
R = 2048
D = 1024
F = 4096
DT = D // P
FT = F // P
ET = D // P
RT = R // P
RB = 256
NRB = R // RB
PRB = 512
NPRB = R // PRB
EPS = 1e-5
N_CORES = 8


class Sem:
    def __init__(self, nc, ctx, name):
        self.h = ctx.enter_context(nc.semaphore(name))
        self.n = 0

    def inc(self, instr, k=1):
        instr.then_inc(self.h, k)
        self.n += k
        return self.n

    def tok(self, instr, k=16):
        return (self, self.inc(instr, k))


def wdma(eng, tok):
    eng.wait_ge(tok[0].h, tok[1])


def emit_ffn(nc, name, src_hbm, dst_hbm, wq1t, wq2t, wq_ld, src_ready,
             ones_row_bf, b1h, b2h_bf, ident_bf, eps_c,
             sL, sS, sT, sV, sA, sHD, sHA, pre_body=None, final_wait=False,
             dbg=None, dbg_base=0, dump_w2_to=None):
    wq1t_v = wq1t[:]
    wq2t_v = wq2t[:]
    with ExitStack() as c:
        xrb = [c.enter_context(nc.sbuf_tensor(f"{name}_x{i}", [P, 2 * D], F32))
               for i in range(2)]
        xn = [c.enter_context(nc.sbuf_tensor(f"{name}_xn{i}", [P, 2 * D], BF16))
              for i in range(2)]
        xnT = [c.enter_context(nc.sbuf_tensor(f"{name}_xnT{i}", [P, DT * RB], BF16))
               for i in range(2)]
        hT = c.enter_context(nc.sbuf_tensor(f"{name}_hT", [P, FT * RB], BF16))
        st = c.enter_context(nc.sbuf_tensor(f"{name}_st", [P, RT * 2], F32))
        vp = c.enter_context(nc.sbuf_tensor(f"{name}_vp", [P, RT], F32))
        rst = c.enter_context(nc.sbuf_tensor(f"{name}_rst", [P, RT], F32))
        ssum = c.enter_context(nc.sbuf_tensor(f"{name}_ssum", [P, RT], F32))
        ssq = c.enter_context(nc.sbuf_tensor(f"{name}_ssq", [P, RT], F32))
        msq = c.enter_context(nc.sbuf_tensor(f"{name}_msq", [P, RT], F32))
        sqscr = c.enter_context(nc.sbuf_tensor(f"{name}_sqscr", [P, D], F32))
        jnkV = c.enter_context(nc.sbuf_tensor(f"{name}_jnkV", [P, 2], F32))
        jnkA = c.enter_context(nc.sbuf_tensor(f"{name}_jnkA", [P, 2], F32))
        ps_h = [c.enter_context(nc.psum_tensor(f"{name}_psh{i}", [P, 512], F32))
                for i in range(2)]
        ps_o = [c.enter_context(nc.psum_tensor(f"{name}_pso{i}", [P, D], F32))
                for i in range(2)]
        ps_x = [c.enter_context(nc.psum_tensor(f"{name}_psx{i}", [P, 1024], BF16))
                for i in range(2)]

        ld_x = [None] * NRB
        var_done = [[0, 0] for _ in range(NRB)]
        ssq_done = [[0, 0] for _ in range(NRB)]
        sq_done = [[0, 0] for _ in range(NRB)]
        xn_done = [[0, 0] for _ in range(NRB)]
        trp_done = [[0, 0] for _ in range(NRB)]
        xnT_done = [[0, 0] for _ in range(NRB)]
        h_mm = [[0] * FT for _ in range(NRB)]
        h_ev = [[0] * FT for _ in range(NRB)]
        o_mm = [[0, 0] for _ in range(NRB)]
        o_ev = [[0, 0] for _ in range(NRB)]
        st_done = [None] * NRB

        with nc.Block(name) as blk:
            if pre_body is not None:
                pre_body(blk)

            def load_x(sync, i):
                for tok in (src_ready or []):
                    wdma(sync, tok)
                if i >= 2:
                    wdma(sync, st_done[i - 2])
                src = src_hbm[i * RB:(i + 1) * RB, :] \
                    .rearrange("(t p) d -> p t d", p=P)
                ins = sync.dma_start(
                    out=xrb[i % 2][:].rearrange("p (t d) -> p t d", t=2), in_=src)
                ld_x[i] = sL[i % 2].tok(ins)

            def s_x01(sync):
                load_x(sync, 0)
                load_x(sync, 1)
            blk.sync(s_x01)

            for i in range(NRB):
                p = i % 2
                for rt in range(2):
                    g = 2 * i + rt
                    def a_ssq(scalar, i=i, rt=rt, g=g, p=p):
                        wdma(scalar, ld_x[i])
                        xt = xrb[p][:, rt * D:(rt + 1) * D]
                        scalar.activation(sqscr[:], xt, AF.Square,
                                          accum_out=ssq[:, g:g + 1])
                        ssq_done[i][rt] = sA.inc(scalar.drain())
                    blk.scalar(a_ssq)

                    def v_bn(vector, i=i, rt=rt, g=g, p=p):
                        wdma(vector, ld_x[i])
                        xt = xrb[p][:, rt * D:(rt + 1) * D]
                        n = sV.inc(vector.reduce_sum(ssum[:, g:g + 1], xt,
                                                     axis=AX.X))
                        vector.wait_ge(sV.h, n)
                        n = sV.inc(vector.tensor_scalar(
                            st[:, 2 * g:2 * g + 1], ssum[:, g:g + 1],
                            1.0 / 1024.0, None, ALU.mult))
                        vector.wait_ge(sV.h, n)
                        n = sV.inc(vector.scalar_tensor_tensor(
                            msq[:, g:g + 1], st[:, 2 * g:2 * g + 1], 4.0,
                            st[:, 2 * g:2 * g + 1], ALU.mult, ALU.mult))
                        vector.wait_ge(sV.h, n)
                        vector.wait_ge(sA.h, ssq_done[i][rt])
                        vector.scalar_tensor_tensor(
                            vp[:, g:g + 1], ssq[:, g:g + 1], 4.0 / 1024.0,
                            msq[:, g:g + 1], ALU.mult, ALU.subtract)
                        var_done[i][rt] = sV.inc(vector.drain())
                    blk.vector(v_bn)

                    def a_sq(scalar, i=i, rt=rt, g=g):
                        scalar.wait_ge(sV.h, var_done[i][rt])
                        scalar.activation(rst[:, g:g + 1], vp[:, g:g + 1],
                                          AF.Sqrt, bias=eps_c[:, 0:1])
                        sq_done[i][rt] = sA.inc(scalar.drain())
                    blk.scalar(a_sq)

                    def v_xn(vector, i=i, rt=rt, g=g, p=p):
                        vector.wait_ge(sA.h, sq_done[i][rt])
                        if i >= 2:
                            vector.wait_ge(sT.h, trp_done[i - 2][1])
                        n = sV.inc(vector.reciprocal(rst[:, g:g + 1],
                                                     rst[:, g:g + 1]))
                        vector.wait_ge(sV.h, n)
                        xt = xrb[p][:, rt * D:(rt + 1) * D]
                        vector.tensor_scalar(
                            xn[p][:, rt * D:(rt + 1) * D], xt,
                            st[:, 2 * g:2 * g + 1], rst[:, g:g + 1],
                            ALU.subtract, ALU.mult)
                        xn_done[i][rt] = sV.inc(vector.drain())
                    blk.vector(v_xn)

                for rt in range(2):
                    def t_tr(tensor, i=i, rt=rt, p=p):
                        tensor.wait_ge(sV.h, xn_done[i][rt])
                        if i >= 1:
                            tensor.wait_ge(sA.h, xnT_done[i - 1][rt])
                        for dt in range(DT):
                            tensor.transpose(
                                ps_x[rt][:, dt * P:(dt + 1) * P],
                                xn[p][:, rt * D + dt * P: rt * D + (dt + 1) * P],
                                ident_bf[:])
                        trp_done[i][rt] = sT.inc(tensor.drain())
                    blk.tensor(t_tr)

                    def a_xnT(scalar, i=i, rt=rt, p=p):
                        scalar.wait_ge(sT.h, trp_done[i][rt])
                        dst = xnT[p][:].rearrange("p (dt r) -> p dt r", dt=DT)[
                            :, :, rt * P:(rt + 1) * P]
                        src = ps_x[rt][:].rearrange("p (dt r) -> p dt r", dt=DT)
                        scalar.copy(dst, src)
                        xnT_done[i][rt] = sA.inc(scalar.drain())
                    blk.scalar(a_xnT)

                for fc in range(FT):
                    def t_mm1(tensor, i=i, fc=fc, p=p):
                        if fc == 0:
                            tensor.wait_ge(sA.h, xnT_done[i][1])
                            if wq_ld is not None:
                                wdma(tensor, wq_ld[0])
                        if fc >= 2:
                            prev = h_ev[i][fc - 2]
                        elif i >= 1:
                            prev = h_ev[i - 1][FT - 2 + fc]
                        else:
                            prev = None
                        if prev is not None:
                            tensor.wait_ge((sHD if fc % 2 == 0 else sHA).h, prev)
                        ins = None
                        for dt in range(DT):
                            ins = tensor.matmul(
                                ps_h[fc % 2][:, 0:RB],
                                wq1t_v[:, dt * F + fc * P: dt * F + (fc + 1) * P],
                                xnT[p][:, dt * RB:(dt + 1) * RB],
                                start=(dt == 0), stop=(dt == DT - 1))
                        h_mm[i][fc] = sT.inc(ins)
                    blk.tensor(t_mm1)

                    if fc % 2 == 0:
                        def v_hev(vector, i=i, fc=fc):
                            vector.wait_ge(sT.h, h_mm[i][fc])
                            ins = vector.tensor_scalar(
                                hT[:, fc * RB:(fc + 1) * RB],
                                ps_h[fc % 2][:, 0:RB],
                                b1h[:, fc:fc + 1], 0.0, ALU.add, ALU.max)
                            h_ev[i][fc] = sHD.inc(ins)
                        blk.vector(v_hev)
                    else:
                        def a_hev(scalar, i=i, fc=fc):
                            scalar.wait_ge(sT.h, h_mm[i][fc])
                            ins = scalar.activation(
                                hT[:, fc * RB:(fc + 1) * RB],
                                ps_h[fc % 2][:, 0:RB], AF.Relu,
                                bias=b1h[:, fc:fc + 1])
                            h_ev[i][fc] = sHA.inc(ins)
                        blk.scalar(a_hev)

                hd_after = sHD.n
                ha_after = sHA.n
                for rt in range(2):
                    def t_mm2(tensor, i=i, rt=rt, hd=hd_after, ha=ha_after):
                        if rt == 0:
                            tensor.wait_ge(sHD.h, hd)
                            tensor.wait_ge(sHA.h, ha)
                            if wq_ld is not None:
                                wdma(tensor, wq_ld[1])
                        if i >= 1:
                            tensor.wait_ge(sV.h, o_ev[i - 1][rt])
                        ins = None
                        for fc in range(FT):
                            for h in range(2):
                                ins = tensor.matmul(
                                    ps_o[rt][:, h * 512:(h + 1) * 512],
                                    hT[:, fc * RB + rt * P: fc * RB + (rt + 1) * P],
                                    wq2t_v[:, fc * D + h * 512: fc * D + (h + 1) * 512],
                                    start=(fc == 0), stop=False)
                        for h in range(2):
                            ins = tensor.matmul(
                                ps_o[rt][:, h * 512:(h + 1) * 512],
                                ones_row_bf[:],
                                b2h_bf[0:1, h * 512:(h + 1) * 512],
                                start=False, stop=True)
                        o_mm[i][rt] = sT.inc(ins)
                    blk.tensor(t_mm2)

                    def v_res(vector, i=i, rt=rt, p=p):
                        vector.wait_ge(sT.h, o_mm[i][rt])
                        xt = xrb[p][:, rt * D:(rt + 1) * D]
                        vector.tensor_tensor(xt, xt, ps_o[rt][:], ALU.add)
                        o_ev[i][rt] = sV.inc(vector.drain())
                    blk.vector(v_res)

                def a_store(scalar, i=i, p=p):
                    scalar.wait_ge(sV.h, o_ev[i][1])
                    dst = dst_hbm[i * RB:(i + 1) * RB, :] \
                        .rearrange("(t p) d -> p t d", p=P)
                    ins = scalar.dma_start(
                        out=dst, in_=xrb[p][:].rearrange("p (t d) -> p t d", t=2))
                    st_done[i] = sS[i % 2].tok(ins)
                blk.scalar(a_store)

                if i + 2 < NRB:
                    def s_pref(sync, i=i):
                        load_x(sync, i + 2)
                    blk.sync(s_pref)

            if dbg is not None:
                def a_dump(scalar):
                    scalar.wait_ge(sV.h, o_ev[NRB - 1][1])
                    b = dbg_base
                    ins = scalar.dma_start(out=dbg[:, b:b + 32], in_=st[:])
                    sS[0].inc(ins, 16)
                    ins = scalar.dma_start(out=dbg[:, b + 32:b + 48], in_=rst[:])
                    sS[0].inc(ins, 16)
                    ins = scalar.dma_start(out=dbg[:, b + 48:b + 64], in_=vp[:])
                    sS[0].inc(ins, 16)
                blk.scalar(a_dump)

            if final_wait:
                final_a = sS[0].n
                final_b = sS[1].n

                def a_final(scalar):
                    scalar.wait_ge(sS[0].h, final_a)
                    scalar.wait_ge(sS[1].h, final_b)
                blk.scalar(a_final)


def build_program(debug=False, last_phase=5):
    nc = bass.Bass("TRN2", target_bir_lowering=False)

    x_in = nc.dram_tensor("x", [R, D], F32, kind="ExternalInput").ap()
    wq_in = nc.dram_tensor("wq", [D, D], F32, kind="ExternalInput").ap()
    wk_in = nc.dram_tensor("wk", [D, D], F32, kind="ExternalInput").ap()
    wv_in = nc.dram_tensor("wv", [D, D], F32, kind="ExternalInput").ap()
    w1_in = nc.dram_tensor("w1", [F, D], F32, kind="ExternalInput").ap()
    b1_in = nc.dram_tensor("b1", [F], F32, kind="ExternalInput").ap()
    w2_in = nc.dram_tensor("w2", [D, F], F32, kind="ExternalInput").ap()
    b2_in = nc.dram_tensor("b2", [D], F32, kind="ExternalInput").ap()
    out_hbm = nc.dram_tensor("out", [R, D], F32, kind="ExternalOutput").ap()

    mid = "ExternalOutput" if debug else "Internal"
    wq1t_hbm = nc.dram_tensor("wq1t_s", [D, F], BF16, kind=mid).ap()
    wq2t_hbm = nc.dram_tensor("wq2t_s", [F, D], BF16, kind=mid).ap()
    x1_hbm = nc.dram_tensor("x1_s", [R, D], F32, kind=mid).ap()
    x2_hbm = nc.dram_tensor("x2_s", [R, D], F32, kind=mid).ap()
    dbg_hbm = nc.dram_tensor("dbg", [P, 288], F32, kind=mid).ap() if debug else None
    dbgw_hbm = nc.dram_tensor("dbgw", [1, 2048], BF16, kind=mid).ap() if debug else None
    wq2s2_hbm = nc.dram_tensor("wq2t_s2", [F, D], BF16, kind=mid).ap() if debug else None

    with ExitStack() as ctx:
        SB = lambda name, shape, dt: ctx.enter_context(nc.sbuf_tensor(name, shape, dt))

        ones_col = SB("ones_col", [P, 1], F32)
        ones_row = SB("ones_row", [1, P], F32)
        ones_row_bf = SB("ones_row_bf", [1, P], BF16)
        ident_bf = SB("ident_bf", [P, P], BF16)
        ident_f32 = SB("ident_f32", [P, P], F32)
        b1h = SB("b1h", [P, FT], F32)
        b2h_bf = SB("b2h_bf", [1, D], BF16)
        s12b = SB("s12b", [P, 2], F32)
        eps_c = SB("eps_c", [P, 2], F32)

        sL = [Sem(nc, ctx, f"sL{i}") for i in range(4)]
        sS = [Sem(nc, ctx, f"sS{i}") for i in range(2)]
        sT = Sem(nc, ctx, "sT")
        sV = Sem(nc, ctx, "sV")
        sA = Sem(nc, ctx, "sA")
        sHD = Sem(nc, ctx, "sHD")
        sHA = Sem(nc, ctx, "sHA")
        sG = Sem(nc, ctx, "sG")

        with ExitStack() as c0:
            gstage = [c0.enter_context(nc.sbuf_tensor(f"gst{i}", [P, 8192], F32))
                      for i in range(2)]
            gacc = c0.enter_context(nc.sbuf_tensor("gacc", [P, 8], F32))
            gtot = c0.enter_context(nc.sbuf_tensor("gtot", [P, 2], F32))
            sinv = c0.enter_context(nc.sbuf_tensor("sinv", [1, 2], F32))
            b1_t32 = c0.enter_context(nc.sbuf_tensor("b1_t32", [32, P], F32))
            b2raw = c0.enter_context(nc.sbuf_tensor("b2raw", [1, D], F32))
            ps0 = c0.enter_context(nc.psum_tensor("ps0", [P, 2], F32))
            ps0b = c0.enter_context(nc.psum_tensor("ps0b", [P, 32], F32))

            w1_c = w1_in.rearrange("(c t p) d -> c p t d", p=P, t=8)
            w2_c = w2_in.rearrange("(c t p) f -> c p t f", p=P, t=2)

            ld_chunk = [None] * 8
            ld_b = [None, None]
            red_done = [0] * 8

            with nc.Block("p0") as blk:
                def load_chunk(sync, k):
                    if k < 4:
                        src = w1_c[k]
                        dst = gstage[k % 2][:].rearrange("p (t d) -> p t d", t=8)
                    else:
                        src = w2_c[k - 4]
                        dst = gstage[k % 2][:].rearrange("p (t f) -> p t f", t=2)
                    ins = sync.dma_start(out=dst, in_=src)
                    ld_chunk[k] = sL[k % 2].tok(ins)

                def s_first(sync):
                    load_chunk(sync, 0)
                    load_chunk(sync, 1)
                    ins = sync.dma_start(
                        out=b1_t32[:], in_=b1_in.rearrange("(c p) -> c p", p=P))
                    ld_b[0] = sL[2].tok(ins)
                    ins = sync.dma_start(out=b2raw[:], in_=b2_in[None, :])
                    ld_b[1] = sL[3].tok(ins)
                blk.sync(s_first)

                def gp_body(gpsimd):
                    gpsimd.memset(ones_col[:], 1.0)
                    gpsimd.memset(ones_row[:], 1.0)
                    gpsimd.memset(ones_row_bf[:], 1.0)
                    gpsimd.memset(eps_c[:, 0:1], 4.0 * EPS)
                    gpsimd.memset(eps_c[:, 1:2], EPS)
                    gpsimd.memset(ident_bf[:], 0.0)
                    n = sG.inc(gpsimd.memset(ident_f32[:], 0.0))
                    gpsimd.wait_ge(sG.h, n)
                    gpsimd.affine_select(
                        out=ident_bf[:], in_=ident_bf[:],
                        compare_op=ALU.not_equal, fill=1.0, base=0,
                        pattern=[[-1, P]], channel_multiplier=1)
                    gpsimd.affine_select(
                        out=ident_f32[:], in_=ident_f32[:],
                        compare_op=ALU.not_equal, fill=1.0, base=0,
                        pattern=[[-1, P]], channel_multiplier=1)
                    sG.inc(gpsimd.drain())
                blk.gpsimd(gp_body)
                g_ready = sG.n

                for k in range(8):
                    def v_red(vector, k=k):
                        wdma(vector, ld_chunk[k])
                        ins = vector.reduce_sum(gacc[:, k:k + 1], gstage[k % 2][:],
                                                axis=AX.X, apply_absolute_value=True)
                        red_done[k] = sV.inc(ins)
                    blk.vector(v_red)
                    if k + 2 < 8:
                        def s_pref(sync, k=k):
                            sync.wait_ge(sV.h, red_done[k])
                            load_chunk(sync, k + 2)
                        blk.sync(s_pref)

                def v_tot(vector):
                    vector.wait_ge(sV.h, red_done[7])
                    vector.reduce_sum(gtot[:, 0:1], gacc[:, 0:4], axis=AX.X)
                    vector.reduce_sum(gtot[:, 1:2], gacc[:, 4:8], axis=AX.X)
                    sV.inc(vector.drain())
                blk.vector(v_tot)
                v_tot_n = sV.n

                def t_sum(tensor):
                    tensor.wait_ge(sV.h, v_tot_n)
                    tensor.wait_ge(sG.h, g_ready)
                    ins = tensor.matmul(ps0[0:1, 0:2], ones_col[:], gtot[:],
                                        start=True, stop=True)
                    sT.inc(ins)
                blk.tensor(t_sum)
                t_sum_n = sT.n

                def v_sinv(vector):
                    vector.wait_ge(sT.h, t_sum_n)
                    n = sV.inc(vector.tensor_copy(sinv[:], ps0[0:1, 0:2]))
                    vector.wait_ge(sV.h, n)
                    n = sV.inc(vector.tensor_scalar(
                        sinv[:], sinv[:], 1.0 / 4194304.0, EPS,
                        ALU.mult, ALU.add))
                    vector.wait_ge(sV.h, n)
                    vector.reciprocal(sinv[:], sinv[:])
                    sV.inc(vector.drain())
                blk.vector(v_sinv)
                v_sinv_n = sV.n

                def t_bc(tensor):
                    tensor.wait_ge(sV.h, v_sinv_n)
                    tensor.matmul(ps0[:, 0:2], ones_row[:], sinv[:],
                                  start=True, stop=True)
                    wdma(tensor, ld_b[0])
                    tensor.transpose(ps0b[:, 0:32], b1_t32[:],
                                     ident_f32[0:32, 0:32])
                    sT.inc(tensor.drain())
                    sT.inc(tensor.nop())
                blk.tensor(t_bc)
                t_bc_n = sT.n

                def v_fin(vector):
                    vector.wait_ge(sT.h, t_bc_n)
                    vector.tensor_copy(s12b[:], ps0[:, 0:2])
                    vector.tensor_scalar(b1h[:], ps0b[:, 0:32], 0.5, None, ALU.mult)
                    wdma(vector, ld_b[1])
                    vector.tensor_scalar(b2h_bf[:], b2raw[:], 0.5, None,
                                         ALU.mult)
                    sV.inc(vector.drain())
                blk.vector(v_fin)
                v_fin_n = sV.n

                if debug:
                    def a_dbg(scalar):
                        scalar.wait_ge(sV.h, v_fin_n)
                        for dst_ap, src_ap in (
                                (dbg_hbm[:, 0:2], s12b[:]),
                                (dbg_hbm[:, 2:34], b1h[:]),
                                (dbg_hbm[:, 34:42], gacc[:]),
                                (dbg_hbm[:, 42:44], gtot[:]),
                                (dbg_hbm[0:1, 44:46], sinv[:])):
                            sS[0].inc(scalar.dma_start(out=dst_ap, in_=src_ap), 16)
                    blk.scalar(a_dbg)

        if last_phase < 1:
            return nc
        with ExitStack() as c12:
            wq1t = c12.enter_context(nc.sbuf_tensor("wq1t", [P, DT * F], BF16))
            wq2t = c12.enter_context(nc.sbuf_tensor("wq2t", [P, FT * D], BF16))
            wq1t_vv = wq1t[:].rearrange("p (dt f) -> p dt f", dt=DT)
            wq2t_vv = wq2t[:].rearrange("p (ft d) -> p ft d", ft=FT)

            with ExitStack() as c1:
                wstage = [c1.enter_context(nc.sbuf_tensor(f"wst{i}", [P, 1024], F32))
                          for i in range(2)]
                wqi = c1.enter_context(nc.sbuf_tensor("wqi", [P, 1024], I32))
                wqn = [c1.enter_context(nc.sbuf_tensor(f"wqn{i}", [P, 1024], BF16))
                       for i in range(2)]
                psw = [c1.enter_context(nc.psum_tensor(f"psw{i}", [P, 1024], BF16))
                       for i in range(2)]

                NTILES = 64
                ld_w = [None] * NTILES
                q_done = [0] * NTILES
                tr_done = [0] * NTILES
                ev_done = [0] * NTILES

                def w_src(t):
                    if t < 32:
                        return w1_in[t * P:(t + 1) * P, :]
                    u = t - 32
                    dt, fs = u // 4, u % 4
                    return w2_in[dt * P:(dt + 1) * P, fs * 1024:(fs + 1) * 1024]

                with nc.Block("p1") as blk:
                    def load_w(sync, t):
                        ins = sync.dma_start(out=wstage[t % 2][:], in_=w_src(t))
                        ld_w[t] = sL[t % 2].tok(ins)

                    def s_first(sync):
                        load_w(sync, 0)
                        load_w(sync, 1)
                    blk.sync(s_first)

                    for t in range(NTILES):
                        def v_quant(vector, t=t):
                            wdma(vector, ld_w[t])
                            if t >= 2:
                                vector.wait_ge(sT.h, tr_done[t - 2])
                            sc = s12b[:, 0:1] if t < 32 else s12b[:, 1:2]
                            n = sV.inc(vector.tensor_scalar(
                                wqi[:], wstage[t % 2][:], sc, None, ALU.mult))
                            vector.wait_ge(sV.h, n)
                            vector.tensor_scalar(wqn[t % 2][:], wqi[:],
                                                 1.0, -1.0, ALU.min, ALU.max)
                            q_done[t] = sV.inc(vector.drain())
                        blk.vector(v_quant)

                        def t_tr(tensor, t=t):
                            tensor.wait_ge(sV.h, q_done[t])
                            if t >= 2:
                                prev = ev_done[t - 2]
                                tensor.wait_ge(
                                    (sHD if (t - 2) % 2 == 0 else sHA).h, prev)
                            for j in range(8):
                                tensor.transpose(
                                    psw[t % 2][:, j * P:(j + 1) * P],
                                    wqn[t % 2][:, j * P:(j + 1) * P],
                                    ident_bf[:])
                            tr_done[t] = sT.inc(tensor.drain())
                        blk.tensor(t_tr)

                        if t < 32:
                            dest = wq1t_vv[:, :, t * P:(t + 1) * P]
                        else:
                            u = t - 32
                            dt, fs = u // 4, u % 4
                            dest = wq2t_vv[:, fs * 8:(fs + 1) * 8, dt * P:(dt + 1) * P]
                        src = psw[t % 2][:].rearrange("p (j c) -> p j c", j=8)

                        if t % 2 == 0:
                            def v_ev(vector, t=t, dest=dest, src=src):
                                vector.wait_ge(sT.h, tr_done[t])
                                vector.tensor_copy(dest, src)
                                ev_done[t] = sHD.inc(vector.drain())
                            blk.vector(v_ev)
                        else:
                            def a_ev(scalar, t=t, dest=dest, src=src):
                                scalar.wait_ge(sT.h, tr_done[t])
                                scalar.copy(dest, src)
                                ev_done[t] = sHA.inc(scalar.drain())
                            blk.scalar(a_ev)

                        if t + 2 < NTILES:
                            def s_pref(sync, t=t):
                                sync.wait_ge(sV.h, q_done[t])
                                load_w(sync, t + 2)
                            blk.sync(s_pref)

                    hd_total = sHD.n
                    ha_total = sHA.n

                    def a_wstore(scalar):
                        scalar.wait_ge(sHD.h, hd_total)
                        scalar.wait_ge(sHA.h, ha_total)
                        ins = scalar.dma_start(
                            out=wq1t_hbm.rearrange("(a p) f -> p a f", p=P),
                            in_=wq1t_vv)
                        sS[0].inc(ins, 16)
                        ins = scalar.dma_start(
                            out=wq2t_hbm.rearrange("(a p) d -> p a d", p=P),
                            in_=wq2t_vv)
                        sS[1].inc(ins, 16)
                    blk.scalar(a_wstore)

            if last_phase < 2:
                return nc
            emit_ffn(nc, "p2", x_in, x1_hbm, wq1t, wq2t, None, None,
                     ones_row_bf, b1h, b2h_bf, ident_bf, eps_c,
                     sL, sS, sT, sV, sA, sHD, sHA,
                     dbg=dbg_hbm, dbg_base=48)

        if last_phase < 3:
            return nc
        with ExitStack() as c34:
            qT = c34.enter_context(nc.sbuf_tensor("qT", [P, ET * R], BF16))
            kT = c34.enter_context(nc.sbuf_tensor("kT", [P, ET * R], BF16))
            vN = c34.enter_context(nc.sbuf_tensor("vN", [P, RT * D], BF16))

            with ExitStack() as c3:
                wqb = c3.enter_context(nc.sbuf_tensor("wqb", [P, DT * D], BF16))
                wkb = c3.enter_context(nc.sbuf_tensor("wkb", [P, DT * D], BF16))
                wvb = c3.enter_context(nc.sbuf_tensor("wvb", [P, DT * D], BF16))
                wst3 = [c3.enter_context(nc.sbuf_tensor(f"wst3{i}", [P, 2048], F32))
                        for i in range(2)]
                xrb3 = [c3.enter_context(nc.sbuf_tensor(f"xrb3{i}", [P, 2 * D], F32))
                        for i in range(2)]
                xn3 = [c3.enter_context(nc.sbuf_tensor(f"xn3{i}", [P, 2 * D], BF16))
                       for i in range(2)]
                xn3T = [c3.enter_context(nc.sbuf_tensor(f"xn3T{i}", [P, DT * RB],
                                                        BF16))
                        for i in range(2)]
                st3 = c3.enter_context(nc.sbuf_tensor("st3", [P, RT * 2], F32))
                vp3 = c3.enter_context(nc.sbuf_tensor("vp3", [P, RT], F32))
                rst3 = c3.enter_context(nc.sbuf_tensor("rst3", [P, RT], F32))
                ssum3 = c3.enter_context(nc.sbuf_tensor("ssum3", [P, RT], F32))
                ssq3 = c3.enter_context(nc.sbuf_tensor("ssq3", [P, RT], F32))
                msq3 = c3.enter_context(nc.sbuf_tensor("msq3", [P, RT], F32))
                sqscr3 = c3.enter_context(nc.sbuf_tensor("sqscr3", [P, D], F32))
                jnkV3 = c3.enter_context(nc.sbuf_tensor("jnkV3", [P, 2], F32))
                jnkA3 = c3.enter_context(nc.sbuf_tensor("jnkA3", [P, 2], F32))
                ps_qk = [c3.enter_context(nc.psum_tensor(f"psqk{i}", [P, 512], F32))
                         for i in range(2)]
                ps_v = [c3.enter_context(nc.psum_tensor(f"psv{i}", [P, D], F32))
                        for i in range(2)]
                ps_x3 = [c3.enter_context(nc.psum_tensor(f"psx3{i}", [P, 1024], BF16))
                         for i in range(2)]

                x1_ready = [(sS[0], sS[0].n), (sS[1], sS[1].n)]
                NWC = 12
                ld_wqkv = [None] * NWC
                cast_done = [0] * NWC
                ld_x1 = [None] * NRB
                xn_done3 = [[0, 0] for _ in range(NRB)]
                var_done3 = [[0, 0] for _ in range(NRB)]
                ssq_done3 = [[0, 0] for _ in range(NRB)]
                sq_done3 = [[0, 0] for _ in range(NRB)]
                trp_done3 = [[0, 0] for _ in range(NRB)]
                xnT_done3 = [[0, 0] for _ in range(NRB)]
                qk_mm = [[0] * 16 for _ in range(NRB)]
                qk_ev = [[0] * 16 for _ in range(NRB)]
                v_mm = [[0, 0] for _ in range(NRB)]
                v_ev = [[0, 0] for _ in range(NRB)]

                with nc.Block("p3") as blk:
                    def load_wqkv(sync, t):
                        w_in_ap = (wq_in, wk_in, wv_in)[t // 4]
                        ch = t % 4
                        src = w_in_ap[ch * 2 * P:(ch + 1) * 2 * P, :] \
                            .rearrange("(a p) d -> p a d", p=P)
                        ins = sync.dma_start(
                            out=wst3[t % 2][:].rearrange("p (a d) -> p a d", a=2),
                            in_=src)
                        ld_wqkv[t] = sL[2 + t % 2].tok(ins)

                    def s_l0(sync):
                        for tok in x1_ready:
                            wdma(sync, tok)
                        load_wqkv(sync, 0)
                        load_wqkv(sync, 1)
                    blk.sync(s_l0)

                    for t in range(NWC):
                        def v_cast(vector, t=t):
                            wdma(vector, ld_wqkv[t])
                            dst = (wqb, wkb, wvb)[t // 4]
                            ch = t % 4
                            vector.tensor_copy(
                                dst[:, ch * 2048:(ch + 1) * 2048],
                                wst3[t % 2][:])
                            cast_done[t] = sV.inc(vector.drain())
                        blk.vector(v_cast)
                        if t + 2 < NWC:
                            def s_pref(sync, t=t):
                                sync.wait_ge(sV.h, cast_done[t])
                                load_wqkv(sync, t + 2)
                            blk.sync(s_pref)

                    def load_x1(sync, j):
                        for tok in x1_ready:
                            wdma(sync, tok)
                        if j >= 2:
                            sync.wait_ge(sV.h, xn_done3[j - 2][1])
                        src = x1_hbm[j * RB:(j + 1) * RB, :] \
                            .rearrange("(t p) d -> p t d", p=P)
                        ins = sync.dma_start(
                            out=xrb3[j % 2][:].rearrange("p (t d) -> p t d", t=2),
                            in_=src)
                        ld_x1[j] = sL[j % 2].tok(ins)

                    def s_x01(sync):
                        load_x1(sync, 0)
                        load_x1(sync, 1)
                    blk.sync(s_x01)

                    for j in range(NRB):
                        p = j % 2
                        for rt in range(2):
                            g = 2 * j + rt
                            def a_ssq(scalar, j=j, rt=rt, g=g, p=p):
                                wdma(scalar, ld_x1[j])
                                xt = xrb3[p][:, rt * D:(rt + 1) * D]
                                scalar.activation(sqscr3[:], xt, AF.Square,
                                                  accum_out=ssq3[:, g:g + 1])
                                ssq_done3[j][rt] = sA.inc(scalar.drain())
                            blk.scalar(a_ssq)

                            def v_bn(vector, j=j, rt=rt, g=g, p=p):
                                wdma(vector, ld_x1[j])
                                xt = xrb3[p][:, rt * D:(rt + 1) * D]
                                n = sV.inc(vector.reduce_sum(
                                    ssum3[:, g:g + 1], xt, axis=AX.X))
                                vector.wait_ge(sV.h, n)
                                n = sV.inc(vector.tensor_scalar(
                                    st3[:, 2 * g:2 * g + 1], ssum3[:, g:g + 1],
                                    1.0 / 1024.0, None, ALU.mult))
                                vector.wait_ge(sV.h, n)
                                n = sV.inc(vector.scalar_tensor_tensor(
                                    msq3[:, g:g + 1], st3[:, 2 * g:2 * g + 1], 1.0,
                                    st3[:, 2 * g:2 * g + 1], ALU.mult, ALU.mult))
                                vector.wait_ge(sV.h, n)
                                vector.wait_ge(sA.h, ssq_done3[j][rt])
                                vector.scalar_tensor_tensor(
                                    vp3[:, g:g + 1], ssq3[:, g:g + 1], 1.0 / 1024.0,
                                    msq3[:, g:g + 1], ALU.mult, ALU.subtract)
                                var_done3[j][rt] = sV.inc(vector.drain())
                            blk.vector(v_bn)

                            def a_sq(scalar, j=j, rt=rt, g=g):
                                scalar.wait_ge(sV.h, var_done3[j][rt])
                                scalar.activation(rst3[:, g:g + 1],
                                                  vp3[:, g:g + 1], AF.Sqrt,
                                                  bias=eps_c[:, 1:2])
                                sq_done3[j][rt] = sA.inc(scalar.drain())
                            blk.scalar(a_sq)

                            def v_xn(vector, j=j, rt=rt, g=g, p=p):
                                vector.wait_ge(sA.h, sq_done3[j][rt])
                                if j >= 2:
                                    vector.wait_ge(sT.h, trp_done3[j - 2][1])
                                n = sV.inc(vector.reciprocal(rst3[:, g:g + 1],
                                                             rst3[:, g:g + 1]))
                                vector.wait_ge(sV.h, n)
                                xt = xrb3[p][:, rt * D:(rt + 1) * D]
                                vector.tensor_scalar(
                                    xn3[p][:, rt * D:(rt + 1) * D], xt,
                                    st3[:, 2 * g:2 * g + 1], rst3[:, g:g + 1],
                                    ALU.subtract, ALU.mult)
                                xn_done3[j][rt] = sV.inc(vector.drain())
                            blk.vector(v_xn)

                        for rt in range(2):
                            def t_tr(tensor, j=j, rt=rt, p=p):
                                tensor.wait_ge(sV.h, xn_done3[j][rt])
                                if j >= 1:
                                    tensor.wait_ge(sA.h, xnT_done3[j - 1][rt])
                                for dt in range(DT):
                                    tensor.transpose(
                                        ps_x3[rt][:, dt * P:(dt + 1) * P],
                                        xn3[p][:, rt * D + dt * P:
                                               rt * D + (dt + 1) * P],
                                        ident_bf[:])
                                trp_done3[j][rt] = sT.inc(tensor.drain())
                            blk.tensor(t_tr)

                            def a_xnT(scalar, j=j, rt=rt, p=p):
                                scalar.wait_ge(sT.h, trp_done3[j][rt])
                                dst = xn3T[p][:].rearrange(
                                    "p (dt r) -> p dt r", dt=DT)[
                                    :, :, rt * P:(rt + 1) * P]
                                src = ps_x3[rt][:].rearrange(
                                    "p (dt r) -> p dt r", dt=DT)
                                scalar.copy(dst, src)
                                xnT_done3[j][rt] = sA.inc(scalar.drain())
                            blk.scalar(a_xnT)

                        for which in range(2):
                            wb = (wqb, wkb)[which]
                            dstT = (qT, kT)[which]
                            for et in range(ET):
                                idx = which * 8 + et
                                def t_qk(tensor, j=j, which=which, et=et, idx=idx,
                                         p=p, wb=wb):
                                    if idx == 0:
                                        tensor.wait_ge(sA.h, xnT_done3[j][1])
                                    tensor.wait_ge(sV.h, cast_done[which * 4 + 3])
                                    if idx >= 2:
                                        tensor.wait_ge(sV.h, qk_ev[j][idx - 2])
                                    elif j >= 1:
                                        tensor.wait_ge(sV.h, qk_ev[j - 1][14 + idx])
                                    ins = None
                                    for dt in range(DT):
                                        ins = tensor.matmul(
                                            ps_qk[idx % 2][:, 0:RB],
                                            wb[:, dt * D + et * P:
                                               dt * D + (et + 1) * P],
                                            xn3T[p][:, dt * RB:(dt + 1) * RB],
                                            start=(dt == 0), stop=(dt == DT - 1))
                                    qk_mm[j][idx] = sT.inc(ins)
                                blk.tensor(t_qk)

                                def v_qkev(vector, j=j, et=et, idx=idx, dstT=dstT):
                                    vector.wait_ge(sT.h, qk_mm[j][idx])
                                    ins = vector.tensor_copy(
                                        dstT[:, et * R + j * RB:
                                             et * R + (j + 1) * RB],
                                        ps_qk[idx % 2][:, 0:RB])
                                    qk_ev[j][idx] = sV.inc(ins)
                                blk.vector(v_qkev)

                        for rt in range(2):
                            def t_v(tensor, j=j, rt=rt, p=p):
                                tensor.wait_ge(sV.h, cast_done[11])
                                if j >= 1:
                                    tensor.wait_ge(sV.h, v_ev[j - 1][rt])
                                ins = None
                                for dt in range(DT):
                                    for h in range(2):
                                        ins = tensor.matmul(
                                            ps_v[rt % 2][:, h * 512:(h + 1) * 512],
                                            xn3T[p][:, dt * RB + rt * P:
                                                   dt * RB + (rt + 1) * P],
                                            wvb[:, dt * D + h * 512:
                                                dt * D + (h + 1) * 512],
                                            start=(dt == 0), stop=(dt == DT - 1))
                                v_mm[j][rt] = sT.inc(ins)
                            blk.tensor(t_v)

                            def v_vev(vector, j=j, rt=rt):
                                vector.wait_ge(sT.h, v_mm[j][rt])
                                g = 2 * j + rt
                                ins = vector.tensor_copy(
                                    vN[:, g * D:(g + 1) * D], ps_v[rt % 2][:])
                                v_ev[j][rt] = sV.inc(ins)
                            blk.vector(v_vev)

                        if j + 2 < NRB:
                            def s_xpref(sync, j=j):
                                load_x1(sync, j + 2)
                            blk.sync(s_xpref)

                    if dbg_hbm is not None:
                        _vlast = v_ev[NRB - 1][1]

                        def a_dump3(scalar):
                            scalar.wait_ge(sV.h, _vlast)
                            ins = scalar.dma_start(out=dbg_hbm[:, 224:256], in_=st3[:])
                            sS[0].inc(ins, 16)
                            ins = scalar.dma_start(out=dbg_hbm[:, 256:272], in_=rst3[:])
                            sS[0].inc(ins, 16)
                        blk.scalar(a_dump3)

            if last_phase < 4:
                return nc
            with ExitStack() as c4:
                x1t = [c4.enter_context(nc.sbuf_tensor(f"x1t{i}", [P, D], F32))
                       for i in range(2)]
                jnkV4 = c4.enter_context(nc.sbuf_tensor("jnkV4", [P, 2], F32))
                jnkA4 = c4.enter_context(nc.sbuf_tensor("jnkA4", [P, 2], F32))
                P_sb = [c4.enter_context(nc.sbuf_tensor(f"Psb{i}", [P, R], BF16))
                        for i in range(2)]
                PT_sb = c4.enter_context(nc.sbuf_tensor("PTsb", [P, R], BF16))
                negmax = c4.enter_context(nc.sbuf_tensor("negmax", [P, RT], F32))
                sumx = c4.enter_context(nc.sbuf_tensor("sumx", [P, RT], F32))
                rs = c4.enter_context(nc.sbuf_tensor("rs", [P, RT], F32))
                ps_s = c4.enter_context(nc.psum_tensor("ps_s", [P, R], F32))
                ps_pt = c4.enter_context(nc.psum_tensor("ps_pt", [P, R], BF16))
                ps_av = c4.enter_context(nc.psum_tensor("ps_av", [P, D], F32))

                NJT = RT
                ld_x1t = [None] * NJT
                sc_done = [0] * NJT
                max_done = [0] * NJT
                exp_done = [0] * NJT
                rs_done = [0] * NJT
                ptt_done = [0] * NJT
                pte_done = [0] * NJT
                av_done = [0] * NJT
                ave_done = [0] * NJT
                st_x2 = [None] * NJT

                with nc.Block("p4") as blk:
                    def load_x1t(sync, jt):
                        if jt >= 2:
                            wdma(sync, st_x2[jt - 2])
                        ins = sync.dma_start(out=x1t[jt % 2][:],
                                             in_=x1_hbm[jt * P:(jt + 1) * P, :])
                        ld_x1t[jt] = sL[jt % 2].tok(ins)

                    def s_x01(sync):
                        load_x1t(sync, 0)
                        load_x1t(sync, 1)
                    blk.sync(s_x01)

                    for jt in range(NJT + 1):
                        if jt >= 1:
                            def t_pt(tensor, jt=jt):
                                tensor.wait_ge(sA.h, exp_done[jt - 1])
                                if jt >= 2:
                                    tensor.wait_ge(sA.h, pte_done[jt - 2])
                                for it in range(RT):
                                    tensor.transpose(
                                        ps_pt[:, it * P:(it + 1) * P],
                                        P_sb[(jt - 1) % 2][:, it * P:(it + 1) * P],
                                        ident_bf[:])
                                ptt_done[jt - 1] = sT.inc(tensor.drain())
                            blk.tensor(t_pt)

                        if jt < NJT:
                            def t_sc(tensor, jt=jt):
                                if jt >= 1:
                                    tensor.wait_ge(sA.h, exp_done[jt - 1])
                                ins = None
                                for seg in range(4):
                                    for et in range(ET):
                                        ins = tensor.matmul(
                                            ps_s[:, seg * 512:(seg + 1) * 512],
                                            kT[:, et * R + jt * P:
                                               et * R + (jt + 1) * P],
                                            qT[:, et * R + seg * 512:
                                               et * R + (seg + 1) * 512],
                                            start=(et == 0), stop=(et == ET - 1))
                                sc_done[jt] = sT.inc(ins)
                            blk.tensor(t_sc)

                            def v_max(vector, jt=jt):
                                vector.wait_ge(sT.h, sc_done[jt])
                                n = sV.inc(vector.reduce_max(
                                    negmax[:, jt:jt + 1], ps_s[:],
                                    axis=AX.X, negate=True))
                                vector.wait_ge(sV.h, n)
                                vector.tensor_scalar(
                                    negmax[:, jt:jt + 1], negmax[:, jt:jt + 1],
                                    1.0 / 32.0, None, ALU.mult)
                                max_done[jt] = sV.inc(vector.drain())
                            blk.vector(v_max)

                        def a_body(scalar, jt=jt):
                            if jt >= 1:
                                scalar.wait_ge(sT.h, ptt_done[jt - 1])
                                if jt >= 2:
                                    scalar.wait_ge(sT.h, av_done[jt - 2])
                                scalar.copy(PT_sb[:], ps_pt[:])
                                pte_done[jt - 1] = sA.inc(scalar.drain())
                            if jt < NJT:
                                scalar.wait_ge(sV.h, max_done[jt])
                                if jt >= 2:
                                    scalar.wait_ge(sT.h, ptt_done[jt - 2])
                                scalar.activation(
                                    P_sb[jt % 2][:], ps_s[:], AF.Exp,
                                    bias=negmax[:, jt:jt + 1], scale=1.0 / 32.0,
                                    accum_out=sumx[:, jt:jt + 1])
                                exp_done[jt] = sA.inc(scalar.drain())
                        blk.scalar(a_body)

                        if jt >= 1:
                            def t_av(tensor, jt=jt):
                                tensor.wait_ge(sA.h, pte_done[jt - 1])
                                if jt >= 2:
                                    tensor.wait_ge(sV.h, ave_done[jt - 2])
                                ins = None
                                for it in range(RT):
                                    for h in range(2):
                                        ins = tensor.matmul(
                                            ps_av[:, h * 512:(h + 1) * 512],
                                            PT_sb[:, it * P:(it + 1) * P],
                                            vN[:, it * D + h * 512:
                                               it * D + (h + 1) * 512],
                                            start=(it == 0), stop=(it == RT - 1))
                                av_done[jt - 1] = sT.inc(ins)
                            blk.tensor(t_av)

                            def v_ave(vector, jt=jt):
                                jj = jt - 1
                                vector.wait_ge(sA.h, exp_done[jj])
                                ins = vector.reciprocal(rs[:, jj:jj + 1],
                                                        sumx[:, jj:jj + 1])
                                rs_done[jj] = sV.inc(ins)
                                vector.wait_ge(sV.h, rs_done[jj])
                                vector.wait_ge(sT.h, av_done[jj])
                                wdma(vector, ld_x1t[jj])
                                vector.scalar_tensor_tensor(
                                    x1t[jj % 2][:], ps_av[:], rs[:, jj:jj + 1],
                                    x1t[jj % 2][:], ALU.mult, ALU.add)
                                ave_done[jj] = sV.inc(vector.drain())
                            blk.vector(v_ave)

                            def a_st(scalar, jt=jt):
                                jj = jt - 1
                                scalar.wait_ge(sV.h, ave_done[jj])
                                ins = scalar.dma_start(
                                    out=x2_hbm[jj * P:(jj + 1) * P, :],
                                    in_=x1t[jj % 2][:])
                                st_x2[jj] = sS[jj % 2].tok(ins)
                            blk.scalar(a_st)

                            if jt + 1 < NJT:
                                def s_pref(sync, jt=jt):
                                    load_x1t(sync, jt + 1)
                                blk.sync(s_pref)

                    if dbg_hbm is not None:
                        _vlast4 = ave_done[NJT - 1]

                        def a_dump4(scalar):
                            scalar.wait_ge(sV.h, _vlast4)
                            ins = scalar.dma_start(out=dbg_hbm[:, 176:192], in_=negmax[:])
                            sS[0].inc(ins, 16)
                            ins = scalar.dma_start(out=dbg_hbm[:, 192:208], in_=sumx[:])
                            sS[0].inc(ins, 16)
                            ins = scalar.dma_start(out=dbg_hbm[:, 208:224], in_=rs[:])
                            sS[0].inc(ins, 16)
                        blk.scalar(a_dump4)

        if last_phase < 5:
            return nc
        with ExitStack() as c5:
            wq1t5 = c5.enter_context(nc.sbuf_tensor("wq1t5", [P, DT * F], BF16))
            wq2t5 = c5.enter_context(nc.sbuf_tensor("wq2t5", [P, FT * D], BF16))
            x2_ready = [(sS[0], sS[0].n), (sS[1], sS[1].n)]
            wq_ld = [None, None]

            def pre_body(blk):
                if dbg_hbm is not None:
                    def a_dumpw(scalar):
                        ins = scalar.dma_start(out=dbgw_hbm[0:1, 0:D], in_=b2h_bf[:])
                        sS[0].inc(ins, 16)
                        ins = scalar.dma_start(out=dbgw_hbm[0:1, D:D + P],
                                               in_=ones_row_bf[:])
                        sS[0].inc(ins, 16)
                    blk.scalar(a_dumpw)

                def s_wq(sync):
                    for tok in x2_ready:
                        wdma(sync, tok)
                    ins = sync.dma_start(
                        out=wq1t5[:].rearrange("p (a f) -> p a f", a=DT),
                        in_=wq1t_hbm.rearrange("(a p) f -> p a f", p=P))
                    wq_ld[0] = sL[2].tok(ins)
                    ins = sync.dma_start(
                        out=wq2t5[:].rearrange("p (a d) -> p a d", a=FT),
                        in_=wq2t_hbm.rearrange("(a p) d -> p a d", p=P))
                    wq_ld[1] = sL[3].tok(ins)
                blk.sync(s_wq)

            emit_ffn(nc, "p5", x2_hbm, out_hbm, wq1t5, wq2t5, wq_ld, x2_ready,
                     ones_row_bf, b1h, b2h_bf, ident_bf, eps_c,
                     sL, sS, sT, sV, sA, sHD, sHA,
                     pre_body=pre_body, final_wait=True,
                     dbg=dbg_hbm, dbg_base=112)

    return nc


_NC_CACHE = {}


def kernel(L0_attn_x, L0_attn_W_query, L0_attn_W_key, L0_attn_W_value,
           L0_ffn_W1, L0_ffn_b1, L0_ffn_W2, L0_ffn_b2):
    x = np.ascontiguousarray(np.asarray(L0_attn_x, dtype=np.float32))
    wq = np.ascontiguousarray(np.asarray(L0_attn_W_query, dtype=np.float32))
    wk = np.ascontiguousarray(np.asarray(L0_attn_W_key, dtype=np.float32))
    wv = np.ascontiguousarray(np.asarray(L0_attn_W_value, dtype=np.float32))
    w1 = np.ascontiguousarray(np.asarray(L0_ffn_W1, dtype=np.float32))
    b1 = np.ascontiguousarray(np.asarray(L0_ffn_b1, dtype=np.float32))
    w2 = np.ascontiguousarray(np.asarray(L0_ffn_W2, dtype=np.float32))
    b2 = np.ascontiguousarray(np.asarray(L0_ffn_b2, dtype=np.float32))

    if "nc" not in _NC_CACHE:
        _NC_CACHE["nc"] = build_program()
    nc = _NC_CACHE["nc"]

    in_maps = [
        {"x": x[c], "wq": wq, "wk": wk, "wv": wv,
         "w1": w1, "b1": b1, "w2": w2, "b2": b2}
        for c in range(N_CORES)
    ]
    res = run_bass_kernel_spmd(nc, in_maps, core_ids=list(range(N_CORES)))
    return np.stack([np.asarray(res.results[c]["out"], dtype=np.float32)
                     for c in range(N_CORES)], axis=0)
